# version 03 (final)
# baseline (speedup 1.0000x reference)
import numpy as np
import concourse.bass as bass
import concourse.mybir as mybir
from concourse.bass_utils import run_bass_kernel_spmd
from contextlib import ExitStack

F32 = mybir.dt.float32
BF16 = mybir.dt.bfloat16
F8E4 = mybir.dt.float8e4
AF = mybir.ActivationFunctionType
ALU = mybir.AluOpType
AX = mybir.AxisListType

D = 2048
C = 4096
I = 8192
T = 2048
TH = 2176
KD = D // 128
NSUB = C // 128
CCH = 128
NCH = C // CCH
JSUB = CCH // 128
NBLK = TH // 128

DEBUG_TAPS = False
PHASES = 99
MIXK = 99
MIXSTAGE = 99
INTERLEAVE_PREP = True


def build_program(debug_taps=False, phases=99, mixk=99, mixstage=99):
    nc = bass.Bass("TRN2", target_bir_lowering=False,
                   detect_race_conditions=False)

    hid = nc.dram_tensor("hid", [TH, D], F32, kind="ExternalInput")
    pre = nc.dram_tensor("pre", [TH, D], F32, kind="ExternalInput")
    smask = nc.dram_tensor("smask", [128, 1], F32, kind="ExternalInput")
    w_inT = nc.dram_tensor("w_inT", [D, C], F32, kind="ExternalInput")
    w_gT = nc.dram_tensor("w_gT", [D, C], F32, kind="ExternalInput")
    w_oT = nc.dram_tensor("w_oT", [C, D], F32, kind="ExternalInput")
    w_fg = nc.dram_tensor("w_fg", [I, D], F32, kind="ExternalInput")
    w_fu = nc.dram_tensor("w_fu", [I, D], F32, kind="ExternalInput")
    w_fd = nc.dram_tensor("w_fd", [D, I], F32, kind="ExternalInput")
    cw4 = nc.dram_tensor("cw4", [128, 128], F32, kind="ExternalInput")
    cbv = nc.dram_tensor("cbv", [128, 32], F32, kind="ExternalInput")
    dt2 = nc.dram_tensor("dt2", [128, 32 * 128], F32, kind="ExternalInput")
    dtbc = nc.dram_tensor("dtbc", [128, 32], F32, kind="ExternalInput")
    alog = nc.dram_tensor("alog", [128, 32], F32, kind="ExternalInput")
    bcv = nc.dram_tensor("bcv", [128, 32], F32, kind="ExternalInput")
    ccv = nc.dram_tensor("ccv", [128, 32], F32, kind="ExternalInput")
    dcv = nc.dram_tensor("dcv", [128, 32], F32, kind="ExternalInput")
    eye = nc.dram_tensor("eye", [128, 128], F32, kind="ExternalInput")

    out = nc.dram_tensor("out", [T, D], F32, kind="ExternalOutput")

    comb_d = nc.dram_tensor("comb_d", [NSUB, 128, T], BF16)
    x2_d = nc.dram_tensor("x2_d", [16, 128, D], F32)
    hqT_d = nc.dram_tensor("hqT_d", [I // 128, 128, T], BF16)

    dbg = {}
    if debug_taps:
        dbg["xnT"] = nc.dram_tensor("dbg_xnT", [128, KD * TH], F32, kind="ExternalOutput")
        dbg["spre"] = nc.dram_tensor("dbg_spre", [128, 32], F32, kind="ExternalOutput")
        dbg["comb"] = nc.dram_tensor("dbg_comb", [NSUB, 128, T], F32, kind="ExternalOutput")
        dbg["x2"] = nc.dram_tensor("dbg_x2", [16, 128, D], F32, kind="ExternalOutput")
        dbg["hqT"] = nc.dram_tensor("dbg_hqT", [I // 128, 128, T], F32, kind="ExternalOutput")
        dbg["xn2qT"] = nc.dram_tensor("dbg_xn2qT", [128, KD * T], F32, kind="ExternalOutput")
        dbg["pipe"] = nc.dram_tensor("dbg_pipe", [7, 128, TH], F32, kind="ExternalOutput")
        dbg["pipe2"] = nc.dram_tensor("dbg_pipe2", [5, 128, D], F32, kind="ExternalOutput")

    ctx = ExitStack()
    with ctx:
        e = ctx.enter_context

        p_cw4 = e(nc.sbuf_tensor("p_cw4", [128, 128], F32))
        p_cb = e(nc.sbuf_tensor("p_cb", [128, 32], F32))
        p_dt2 = e(nc.sbuf_tensor("p_dt2", [128, 32 * 128], BF16))
        p_dtbc = e(nc.sbuf_tensor("p_dtbc", [128, 32], F32))
        p_ac = e(nc.sbuf_tensor("p_ac", [128, 32], F32))
        p_bc = e(nc.sbuf_tensor("p_bc", [128, 32], F32))
        p_cc = e(nc.sbuf_tensor("p_cc", [128, 32], F32))
        p_dc = e(nc.sbuf_tensor("p_dc", [128, 32], F32))
        p_sm = e(nc.sbuf_tensor("p_sm", [128, 1], F32))
        p_eye = e(nc.sbuf_tensor("p_eye", [128, 128], BF16))
        spre = e(nc.sbuf_tensor("spre", [128, 32], F32))
        s0 = e(nc.sbuf_tensor("s0", [128, 32], F32))
        zcol = e(nc.sbuf_tensor("zcol", [128, 1], F32))
        stl = e(nc.sbuf_tensor("stl", [128, 1], F32))

        sems = {}
        for name in ["dPrm", "dPrmG", "dX", "dW", "dWG", "dS", "mmP", "mmG", "mmD", "mmT",
                     "aEv", "aSil", "aSpA", "aA", "aSig", "aSm", "aTrE",
                     "vXb", "vSpD", "vDtb", "vScan", "vCmb", "vUT", "vXn",
                     "vSm", "vEvO", "pCv", "oDone", "dCmb"]:
            sems[name] = e(nc.semaphore(name))
        cnt = dict.fromkeys(sems, 0)

        def inc(instr, name, k=1):
            instr.then_inc(sems[name], k)
            cnt[name] += k
            return (name, cnt[name])

        def snap(name):
            return (name, cnt[name])

        def wt(eng, guard):
            if guard is None:
                return
            name, v = guard
            if v > 0:
                eng.wait_ge(sems[name], v)

        V, A, P, G, SY = nc.vector, nc.scalar, nc.tensor, nc.gpsimd, nc.sync

        gp0 = None
        for (dst, src) in [(p_cw4, cw4), (p_cb, cbv), (p_dtbc, dtbc),
                           (p_ac, alog), (p_bc, bcv), (p_cc, ccv),
                           (p_dc, dcv), (p_sm, smask)]:
            gp0 = inc(SY.dma_start(out=dst[:], in_=src[:, :]), "dPrm", 16)
        gp1 = inc(G.dma_start(out=p_dt2[:], in_=dt2[:, :]), "dPrmG", 16)
        gp1 = inc(G.dma_start(out=p_eye[:], in_=eye[:, :]), "dPrmG", 16)
        for eng in (A, V, P, G):
            wt(eng, gp0)
            wt(eng, gp1)
        A.activation(out=spre[:], in_=p_ac[:], func=AF.Exp)
        gac = inc(A.activation(out=p_ac[:], in_=spre[:], func=AF.Copy,
                               scale=-1.0), "aSm")
        V.memset(zcol[:], 0.0)
        nc.all_engine_barrier()

        def emit_xnT(src, xnT, ptr, nblk, lnbufs):
            (xt, scrD, scrA, sm1, sq1, mv, qv, m2, va, lv, rs, xnb) = lnbufs
            if True:
                xt_g = [None, None]
                xnb_g = [None, None]
                ptr_g = [None, None]
                TW = xnT.ap().shape[1] // KD
                for blk in range(nblk):
                    par = blk % 2
                    wt(SY, xt_g[par])
                    gld = inc(SY.dma_start(out=xt[par][:],
                                           in_=src[blk * 128:(blk + 1) * 128, :]),
                              "dX", 16)
                    wt(V, gld)
                    gsum = inc(V.tensor_scalar(out=scrD[:], in0=xt[par][:],
                                               scalar1=1.0, scalar2=0.0,
                                               op0=ALU.mult, op1=ALU.add,
                                               accum_out=sm1[par][:, 0:1]), "vSm")
                    wt(A, gld)
                    A.activation(out=scrA[:], in_=xt[par][:], func=AF.Square,
                                 accum_out=sq1[par][:, 0:1])
                    wt(A, gsum)
                    A.activation(out=mv[par][:], in_=sm1[par][:], func=AF.Copy,
                                 scale=1.0 / 2048.0)
                    A.activation(out=qv[par][:], in_=sq1[par][:], func=AF.Copy,
                                 scale=1.0 / 2048.0)
                    gm2 = inc(A.activation(out=m2[par][:], in_=mv[par][:],
                                           func=AF.Square), "aSm")
                    wt(V, gm2)
                    gvar = inc(V.scalar_tensor_tensor(
                        out=va[par][:], in0=qv[par][:], scalar=1e-5,
                        in1=m2[par][:], op0=ALU.add, op1=ALU.subtract), "vSm")
                    wt(A, gvar)
                    A.activation(out=lv[par][:], in_=va[par][:], func=AF.Ln)
                    grst = inc(A.activation(out=rs[par][:], in_=lv[par][:],
                                            func=AF.Exp, scale=-0.5), "aSm")
                    wt(V, grst)
                    wt(V, xnb_g[par])
                    gxn = inc(V.tensor_scalar(out=xnb[par][:], in0=xt[par][:],
                                              scalar1=mv[par][:, 0:1],
                                              scalar2=rs[par][:, 0:1],
                                              op0=ALU.subtract, op1=ALU.mult), "vXn")
                    xt_g[par] = gxn
                    evg = None
                    for q in range(4):
                        pq_ = (blk * 4 + q) % 2
                        wt(P, gxn)
                        wt(P, ptr_g[pq_])
                        for u in range(4):
                            k = q * 4 + u
                            mm = P.transpose(ptr[pq_][:, u * 128:(u + 1) * 128],
                                             xnb[par][:, k * 128:(k + 1) * 128],
                                             p_eye[:])
                        gtr = inc(mm, "mmT")
                        wt(A, gtr)
                        dst = xnT[:].rearrange("p (a t) -> p a t", t=TW)[
                            :, q * 4:(q + 1) * 4, blk * 128:(blk + 1) * 128]
                        evg = inc(A.activation(out=dst, in_=ptr[pq_][:].rearrange(
                            "p (a t) -> p a t", t=128), func=AF.Copy), "aTrE")
                        ptr_g[pq_] = evg
                    xnb_g[par] = evg

        def emit_mixer(xnT, prefix, bufs):
            (winT, wgT, pp, pq, pdt, proj_buf, cA, cB, cX, xtile, xtile_b,
             spM, spL, spN, dtb_sb, a_buf, u_buf, s_sb, g_sb, t1_sb, t2_sb,
             comb_sb) = bufs
            win_ld = [None] * NCH
            wg_ld = [None] * NCH
            win_g = [None, None]
            wg_g = [None, None]
            pp_g = [None, None]
            pq_g = [None, None]
            pdt_g = [[], []]
            pb_g = [None, None]
            cX_g = [None, None]
            xt_g = [None, None]
            spM_g = [None, None]
            spL_g = [None, None]
            dtb_g = [None, None]
            a_g = [None, None]
            cmb_g = [None, None]

            srcv = w_inT.ap().rearrange("(a p) c -> p a c", p=128)
            srcg = w_gT.ap().rearrange("(a p) c -> p a c", p=128)
            pend = [None]

            def emit_ystage(p_):
                gp_, s_, tt_, gsig_ = p_
                V.tensor_scalar(out=t1_sb[gp_][:], in0=xtile[gp_][:],
                                scalar1=p_dc[:, s_:s_ + 1],
                                scalar2=None, op0=ALU.mult)
                gut = inc(V.scalar_tensor_tensor(
                    out=t2_sb[gp_][:], in0=s_sb[gp_][:, 1:513],
                    scalar=p_cc[:, s_:s_ + 1], in1=t1_sb[gp_][:],
                    op0=ALU.mult, op1=ALU.add), "vUT")
                xt_g[gp_] = gut
                wt(V, gsig_)
                wt(V, cmb_g[gp_])
                gcmb = inc(V.tensor_tensor(out=comb_sb[gp_][:],
                                           in0=t2_sb[gp_][:],
                                           in1=g_sb[gp_][:],
                                           op=ALU.mult), "vCmb")
                wt(G, gcmb)
                cmb_g[gp_] = inc(G.dma_start(
                    out=comb_d[s_, :, (tt_ - 1) * 512:tt_ * 512],
                    in_=comb_sb[gp_][:]), "dCmb", 16)

            def load_chunk(k):
                kb = k % 2
                wt(G, win_g[kb])
                win_ld[k] = inc(G.dma_start(
                    out=winT[kb][:].rearrange("p (a c) -> p a c", c=CCH),
                    in_=srcv[:, :, k * CCH:(k + 1) * CCH]), "dWG", 16)
                if not prefix:
                    wt(G, wg_g[kb])
                    wg_ld[k] = inc(G.dma_start(
                        out=wgT[kb][:].rearrange("p (a c) -> p a c", c=CCH),
                        in_=srcg[:, :, k * CCH:(k + 1) * CCH]), "dWG", 16)

            load_chunk(0)
            load_chunk(1)
            ng = 0
            nip = 0
            for k in range(min(NCH, mixk)):
                kb = k % 2
                for j in range(JSUB):
                    s = k * JSUB + j
                    pbs = proj_buf[s % 2]
                    for tt in range(5):
                        par = nip % 2
                        t0 = 0 if tt == 0 else 128 + (tt - 1) * 512
                        tw = 128 if tt == 0 else 512
                        gp = ng % 2
                        wt(P, win_ld[k])
                        wt(P, pp_g[par])
                        for kd in range(KD):
                            mm = P.matmul(
                                pp[par][:, 0:tw],
                                winT[kb][:].rearrange("p (a c) -> p a c", c=CCH)[
                                    :, kd, j * 128:(j + 1) * 128],
                                xnT[:].rearrange("p (a t) -> p a t", t=TH)[
                                    :, kd, t0:t0 + tw],
                                start=(kd == 0), stop=(kd == KD - 1))
                        gmp = inc(mm, "mmP")
                        if (not prefix) and tt > 0:
                            wt(P, wg_ld[k])
                            wt(P, pq_g[gp])
                            for kd in range(KD):
                                mm = P.matmul(
                                    pq[gp][:],
                                    wgT[kb][:].rearrange("p (a c) -> p a c", c=CCH)[
                                        :, kd, j * 128:(j + 1) * 128],
                                    xnT[:].rearrange("p (a t) -> p a t", t=TH)[
                                        :, kd, t0:t0 + tw],
                                    start=(kd == 0), stop=(kd == KD - 1))
                            gmg = inc(mm, "mmG")
                        wt(A, gmp)
                        wt(A, pb_g[s % 2] if tt == 0 else None)
                        gev = inc(A.activation(out=pbs[:, t0:t0 + tw],
                                               in_=pp[par][:, 0:tw], func=AF.Copy),
                                  "aEv")
                        pp_g[par] = gev
                        if tt > 0 and mixstage >= 2:
                            wt(V, gev)
                            wt(V, cX_g[gp])
                            o = t0 - 3
                            V.tensor_scalar(out=cA[:], in0=pbs[:, o:o + 512],
                                            scalar1=p_cw4[:, s * 4:s * 4 + 1],
                                            scalar2=p_cb[:, s:s + 1],
                                            op0=ALU.mult, op1=ALU.add)
                            V.scalar_tensor_tensor(out=cB[:], in0=pbs[:, o + 1:o + 513],
                                                   scalar=p_cw4[:, s * 4 + 1:s * 4 + 2],
                                                   in1=cA[:], op0=ALU.mult, op1=ALU.add)
                            V.scalar_tensor_tensor(out=cA[:], in0=pbs[:, o + 2:o + 514],
                                                   scalar=p_cw4[:, s * 4 + 2:s * 4 + 3],
                                                   in1=cB[:], op0=ALU.mult, op1=ALU.add)
                            gcv = inc(V.scalar_tensor_tensor(
                                out=cX[gp][:], in0=pbs[:, o + 3:o + 515],
                                scalar=p_cw4[:, s * 4 + 3:s * 4 + 4],
                                in1=cA[:], op0=ALU.mult, op1=ALU.add), "pCv")
                            if tt == 4:
                                pb_g[s % 2] = gcv
                        if tt > 0 and mixstage >= 3:
                            gp = ng % 2
                            wt(A, gcv)
                            wt(A, xt_g[gp])
                            gsl = inc(A.activation(out=xtile[gp][:], in_=cX[gp][:],
                                                   func=AF.Silu), "aSil")
                            cX_g[gp] = gsl
                            wt(V, gsl)
                            xb_rdy = inc(V.tensor_copy(out=xtile_b[gp][:],
                                                       in_=xtile[gp][:]), "vXb")
                        if tt > 0 and mixstage >= 4:
                            gp = ng % 2
                            wt(P, xb_rdy)
                            for gg in pdt_g[gp]:
                                wt(P, gg)
                            gmd = inc(P.matmul(pdt[gp][:],
                                               p_dt2[:, s * 128:(s + 1) * 128],
                                               xtile_b[gp][:], start=True,
                                               stop=True), "mmD")
                            wt(V, gmd)
                            wt(V, spM_g[gp])
                            gsp1 = inc(V.tensor_scalar(out=spM[gp][:], in0=pdt[gp][:],
                                                       scalar1=p_dtbc[:, s:s + 1],
                                                       scalar2=0.0, op0=ALU.add,
                                                       op1=ALU.max), "vSpD")
                            wt(A, gmd)
                            wt(A, gsp1)
                            wt(A, spL_g[gp])
                            A.activation(out=spL[gp][:], in_=pdt[gp][:], func=AF.Abs,
                                         bias=p_dtbc[:, s:s + 1])
                            A.activation(out=spN[gp][:], in_=spL[gp][:], func=AF.Exp,
                                         scale=-1.0)
                            gsp4 = inc(A.activation(out=spL[gp][:], in_=spN[gp][:],
                                                    func=AF.Ln, bias=1.0), "aSpA")
                            pdt_g[gp] = [gsp4, gsp1]
                            wt(V, gsp4)
                            wt(V, dtb_g[gp])
                            gdtb = inc(V.tensor_add(dtb_sb[gp][:], spM[gp][:],
                                                    spL[gp][:]), "vDtb")
                            spM_g[gp] = gdtb
                            spL_g[gp] = gdtb
                        if tt > 0 and mixstage >= 5:
                            gp = ng % 2
                            wt(A, gdtb)
                            wt(A, a_g[gp])
                            ga = inc(A.activation(out=a_buf[gp][:, 1:513],
                                                  in_=dtb_sb[gp][:], func=AF.Exp,
                                                  scale=p_ac[:, s:s + 1]), "aA")
                            dtb_g[gp] = ga
                        if tt > 0 and mixstage >= 6:
                            gp = ng % 2
                            gu = V.scalar_tensor_tensor(out=u_buf[gp][:, 1:513],
                                                        in0=dtb_sb[gp][:],
                                                        scalar=p_bc[:, s:s + 1],
                                                        in1=xtile[gp][:],
                                                        op0=ALU.mult, op1=ALU.mult)
                            if prefix:
                                xt_g[gp] = inc(gu, "vUT")
                            if tt == 1:
                                src_c = zcol[:, 0:1] if prefix else s0[:, s:s + 1]
                                V.tensor_copy(out=u_buf[gp][:, 0:1], in_=src_c)
                            else:
                                V.tensor_copy(out=u_buf[gp][:, 0:1],
                                              in_=s_sb[(ng - 1) % 2][:, 512:513])
                            wt(V, ga)
                            gsc = inc(V.tensor_tensor_scan(out=s_sb[gp][:],
                                                           data0=a_buf[gp][:],
                                                           data1=u_buf[gp][:],
                                                           initial=0.0, op0=ALU.mult,
                                                           op1=ALU.add), "vScan")
                            a_g[gp] = gsc
                            if prefix:
                                if tt == 4:
                                    wt(G, gsc)
                                    G.tensor_copy(out=spre[:, s:s + 1],
                                                  in_=s_sb[gp][:, 512:513])
                            else:
                                wt(A, gmg)
                                gsig = inc(A.activation(out=g_sb[gp][:],
                                                        in_=pq[gp][:],
                                                        func=AF.Sigmoid), "aSig")
                                pq_g[gp] = gsig
                                if pend[0] is not None:
                                    emit_ystage(pend[0])
                                pend[0] = (gp, s, tt, gsig)
                        if tt > 0:
                            ng += 1
                        nip += 1
                win_g[kb] = snap("mmP")
                if not prefix:
                    wg_g[kb] = snap("mmG")
                if k + 2 < min(NCH, mixk):
                    load_chunk(k + 2)
            if (not prefix) and pend[0] is not None:
                emit_ystage(pend[0])
                pend[0] = None

        if phases < 1:
            return nc
        with ExitStack() as c2:
            e2 = c2.enter_context
            xnT = e2(nc.sbuf_tensor("xnT", [128, KD * TH], BF16))
            ptrB = [e2(nc.psum_tensor(f"ptrB{i}", [128, 512], BF16)) for i in range(2)]
            ln_xt = [e2(nc.sbuf_tensor(f"ln_xt{i}", [128, D], F32)) for i in range(2)]
            ln_scrD = e2(nc.sbuf_tensor("ln_scrD", [128, D], F32))
            ln_scrA = e2(nc.sbuf_tensor("ln_scrA", [128, D], F32))
            ln_sm = [e2(nc.sbuf_tensor(f"ln_sm{i}", [128, 1], F32)) for i in range(2)]
            ln_sq = [e2(nc.sbuf_tensor(f"ln_sq{i}", [128, 1], F32)) for i in range(2)]
            ln_mv = [e2(nc.sbuf_tensor(f"ln_mv{i}", [128, 1], F32)) for i in range(2)]
            ln_qv = [e2(nc.sbuf_tensor(f"ln_qv{i}", [128, 1], F32)) for i in range(2)]
            ln_m2 = [e2(nc.sbuf_tensor(f"ln_m2{i}", [128, 1], F32)) for i in range(2)]
            ln_va = [e2(nc.sbuf_tensor(f"ln_va{i}", [128, 1], F32)) for i in range(2)]
            ln_lv = [e2(nc.sbuf_tensor(f"ln_lv{i}", [128, 1], F32)) for i in range(2)]
            ln_rs = [e2(nc.sbuf_tensor(f"ln_rs{i}", [128, 1], F32)) for i in range(2)]
            ln_xnb = [e2(nc.sbuf_tensor(f"ln_xnb{i}", [128, D], BF16)) for i in range(2)]
            lnbufs = (ln_xt, ln_scrD, ln_scrA, ln_sm, ln_sq, ln_mv, ln_qv,
                      ln_m2, ln_va, ln_lv, ln_rs, ln_xnb)

            emit_xnT(pre, xnT, ptrB, NBLK, lnbufs)
            nc.all_engine_barrier()
            if debug_taps and phases == 1:
                G.dma_start(out=dbg["xnT"][:, :], in_=xnT[:]).then_inc(
                    sems["dS"], 16)
                cnt["dS"] += 16
                wt(G, snap("dS"))
            if phases < 2:
                return nc

            with ExitStack() as c3:
                e3 = c3.enter_context
                winT = [e3(nc.sbuf_tensor(f"winT{i}", [128, KD * CCH], BF16))
                        for i in range(2)]
                wgT = [e3(nc.sbuf_tensor(f"wgT{i}", [128, KD * CCH], BF16))
                       for i in range(2)]
                proj_buf = [e3(nc.sbuf_tensor(f"projb{i}", [128, TH], F32))
                            for i in range(2)]
                cA = e3(nc.sbuf_tensor("cA", [128, 512], F32))
                cB = e3(nc.sbuf_tensor("cB", [128, 512], F32))
                cX = [e3(nc.sbuf_tensor(f"cX{i}", [128, 512], F32)) for i in range(2)]
                xtile = [e3(nc.sbuf_tensor(f"xtile{i}", [128, 512], F32))
                         for i in range(2)]
                xtile_b = [e3(nc.sbuf_tensor(f"xtileb{i}", [128, 512], BF16))
                           for i in range(2)]
                spM = [e3(nc.sbuf_tensor(f"spM{i}", [128, 512], F32)) for i in range(2)]
                spL = [e3(nc.sbuf_tensor(f"spL{i}", [128, 512], F32)) for i in range(2)]
                spN = [e3(nc.sbuf_tensor(f"spN{i}", [128, 512], F32)) for i in range(2)]
                dtb_sb = [e3(nc.sbuf_tensor(f"dtb{i}", [128, 512], F32))
                          for i in range(2)]
                a_buf = [e3(nc.sbuf_tensor(f"abuf{i}", [128, 513], F32))
                         for i in range(2)]
                u_buf = [e3(nc.sbuf_tensor(f"ubuf{i}", [128, 513], F32))
                         for i in range(2)]
                s_sb = [e3(nc.sbuf_tensor(f"ssb{i}", [128, 513], F32))
                        for i in range(2)]
                g_sb = [e3(nc.sbuf_tensor(f"gsb{i}", [128, 512], F32))
                        for i in range(2)]
                t1_sb = [e3(nc.sbuf_tensor(f"t1sb{i}", [128, 512], F32))
                         for i in range(2)]
                t2_sb = [e3(nc.sbuf_tensor(f"t2sb{i}", [128, 512], F32))
                         for i in range(2)]
                comb_sb = [e3(nc.sbuf_tensor(f"cmbsb{i}", [128, 512], BF16))
                           for i in range(2)]
                pp = [e3(nc.psum_tensor(f"pp{i}", [128, 512], F32)) for i in range(2)]
                pdt = [e3(nc.psum_tensor(f"pdt{i}", [128, 512], F32)) for i in range(2)]
                pq = [e3(nc.psum_tensor(f"pq{i}", [128, 512], F32)) for i in range(2)]
                for i in range(2):
                    V.memset(a_buf[i][:, 0:1], 0.0)
                nc.all_engine_barrier()

                mixbufs = (winT, wgT, pp, pq, pdt, proj_buf, cA, cB, cX,
                           xtile, xtile_b, spM, spL, spN, dtb_sb, a_buf, u_buf,
                           s_sb, g_sb, t1_sb, t2_sb, comb_sb)
                emit_mixer(xnT, True, mixbufs)
                nc.all_engine_barrier()
                if debug_taps and phases == 2:
                    for di, (buf, w) in enumerate([
                            (proj_buf[1], TH), (cX[1], 512), (xtile[1], 512),
                            (dtb_sb[1], 512), (a_buf[1], 513), (u_buf[1], 513),
                            (s_sb[1], 513)]):
                        G.dma_start(out=dbg["pipe"][di, :, 0:w],
                                    in_=buf[:, 0:w]).then_inc(sems["dS"], 16)
                        cnt["dS"] += 16
                    wt(G, snap("dS"))
                    nc.all_engine_barrier()
                gs0 = inc(V.tensor_scalar(out=s0[:], in0=spre[:],
                                          scalar1=p_sm[:, 0:1], scalar2=None,
                                          op0=ALU.mult), "vSm")
                if debug_taps:
                    wt(G, gs0)
                    G.dma_start(out=dbg["spre"][:, :], in_=s0[:]).then_inc(
                        sems["dS"], 16)
                    cnt["dS"] += 16
                    wt(G, snap("dS"))
                nc.all_engine_barrier()
                if phases < 3:
                    return nc

                emit_xnT(hid, xnT, ptrB, NBLK, lnbufs)
                nc.all_engine_barrier()
                if debug_taps:
                    G.dma_start(out=dbg["xnT"][:, :], in_=xnT[:]).then_inc(
                        sems["dS"], 16)
                    cnt["dS"] += 16
                    nc.all_engine_barrier()
                if phases < 4:
                    return nc

                emit_mixer(xnT, False, mixbufs)
                wt(SY, snap("dCmb"))
                nc.all_engine_barrier()
                if debug_taps:
                    for sI in range(NSUB):
                        G.dma_start(out=dbg["comb"][sI, :, :],
                                    in_=comb_d[sI, :, :]).then_inc(sems["dS"], 16)
                        cnt["dS"] += 16
                    wt(G, snap("dS"))
                    nc.all_engine_barrier()
        if phases < 5:
            return nc

        with ExitStack() as c5:
            e5 = c5.enter_context
            woT = e5(nc.sbuf_tensor("woT", [128, NSUB * 1024], BF16))
            cmb = [e5(nc.sbuf_tensor(f"cmb{i}", [128, NSUB * 128], BF16))
                   for i in range(2)]
            x_in = [e5(nc.sbuf_tensor(f"xin{i}", [128, 1024], F32)) for i in range(2)]
            x2h = [e5(nc.sbuf_tensor(f"x2h{i}", [128, 1024], F32)) for i in range(2)]
            po = [e5(nc.psum_tensor(f"po{i}", [128, 512], F32)) for i in range(2)]
            srco = w_oT.ap().rearrange("(a p) e -> p a e", p=128)
            cmb_g = [None, None]
            xin_g = [None, None]
            x2h_g = [None, None]
            po_g = [None, None]
            wo_g = None
            for eh in range(2):
                wt(G, wo_g)
                gwo = inc(G.dma_start(
                    out=woT[:].rearrange("p (a e) -> p a e", e=1024),
                    in_=srco[:, :, eh * 1024:(eh + 1) * 1024]), "dWG", 16)
                for i in range(16):
                    par = i % 2
                    wt(SY, cmb_g[par])
                    gcm = inc(SY.dma_start(
                        out=cmb[par][:].rearrange("p (a t) -> p a t", t=128),
                        in_=comb_d.ap().rearrange("s p t -> p s t")[
                            :, :, i * 128:(i + 1) * 128]), "dX", 16)
                    wt(SY, xin_g[par])
                    gxi = inc(SY.dma_start(
                        out=x_in[par][:],
                        in_=hid[128 + i * 128:256 + i * 128,
                                eh * 1024:(eh + 1) * 1024]), "dX", 16)
                    wt(P, gwo)
                    wt(P, gcm)
                    for et in range(2):
                        wt(P, po_g[et])
                        for sI in range(NSUB):
                            mm = P.matmul(
                                po[et][:],
                                cmb[par][:, sI * 128:(sI + 1) * 128],
                                woT[:].rearrange("p (a e) -> p a e", e=1024)[
                                    :, sI, et * 512:(et + 1) * 512],
                                start=(sI == 0), stop=(sI == NSUB - 1))
                        gmo = inc(mm, "mmP")
                        if et == 1:
                            cmb_g[par] = gmo
                        wt(V, gmo)
                        wt(V, gxi)
                        wt(V, x2h_g[par] if et == 0 else None)
                        gev = inc(V.tensor_add(x2h[par][:, et * 512:(et + 1) * 512],
                                               po[et][:],
                                               x_in[par][:, et * 512:(et + 1) * 512]),
                                  "vEvO")
                        po_g[et] = gev
                        if et == 1:
                            xin_g[par] = gev
                    wt(G, gev)
                    x2h_g[par] = inc(G.dma_start(
                        out=x2_d[i, :, eh * 1024:(eh + 1) * 1024],
                        in_=x2h[par][:]), "dS", 16)
                wo_g = snap("mmP")
            wt(SY, snap("dS"))
            nc.all_engine_barrier()
            if debug_taps:
                for i in range(16):
                    G.dma_start(out=dbg["x2"][i, :, :], in_=x2_d[i, :, :]).then_inc(
                        sems["dS"], 16)
                    cnt["dS"] += 16
                wt(G, snap("dS"))
                nc.all_engine_barrier()
        if phases < 6:
            return nc

        with ExitStack() as c6o:
            e6o = c6o.enter_context
            xn2qT = e6o(nc.sbuf_tensor("xn2qT", [128, KD * T], BF16))

            with ExitStack() as c6:
                e6 = c6.enter_context
                xt = [e6(nc.sbuf_tensor(f"f2xt{i}", [128, D], F32)) for i in range(2)]
                scrD = e6(nc.sbuf_tensor("f2scrD", [128, D], F32))
                scrA = e6(nc.sbuf_tensor("f2scrA", [128, D], F32))
                xn2 = [e6(nc.sbuf_tensor(f"xn2_{i}", [128, D], F32)) for i in range(2)]
                am = e6(nc.sbuf_tensor("f2am", [128, 64], F32))
                rs2 = e6(nc.sbuf_tensor("f2rs", [128, 64], F32))
                rs2b = e6(nc.sbuf_tensor("f2rsb", [128, 64], F32))
                rs2c = e6(nc.sbuf_tensor("f2rsc", [128, 64], F32))
                rs2f = e6(nc.sbuf_tensor("f2rsf", [128, 64], F32))
                ts2f = e6(nc.sbuf_tensor("f2tsf", [128, 64], F32))
                ts2 = e6(nc.sbuf_tensor("f2ts", [128, 64], F32))
                q8 = e6(nc.sbuf_tensor("f2q8", [128, D], F8E4))
                xq = [e6(nc.sbuf_tensor(f"f2xq{i}", [128, D], BF16)) for i in range(2)]
                sm1 = [e6(nc.sbuf_tensor(f"f2sm{i}", [128, 1], F32)) for i in range(2)]
                sq1 = [e6(nc.sbuf_tensor(f"f2sq{i}", [128, 1], F32)) for i in range(2)]
                mv = [e6(nc.sbuf_tensor(f"f2mv{i}", [128, 1], F32)) for i in range(2)]
                qv = [e6(nc.sbuf_tensor(f"f2qv{i}", [128, 1], F32)) for i in range(2)]
                m2 = [e6(nc.sbuf_tensor(f"f2m2{i}", [128, 1], F32)) for i in range(2)]
                va = [e6(nc.sbuf_tensor(f"f2va{i}", [128, 1], F32)) for i in range(2)]
                lv = [e6(nc.sbuf_tensor(f"f2lv{i}", [128, 1], F32)) for i in range(2)]
                rst = [e6(nc.sbuf_tensor(f"f2rst{i}", [128, 1], F32)) for i in range(2)]
                ptrF = [e6(nc.psum_tensor(f"ptrF{i}", [128, 512], BF16))
                        for i in range(2)]
                xt_g = [None, None]
                xq_g = [None, None]
                ptr_g = [None, None]
                rb_g = [None]
                f2f_g = [None]
                for blk in range(16):
                    par = blk % 2
                    wt(SY, xt_g[par])
                    gld = inc(SY.dma_start(out=xt[par][:], in_=x2_d[blk, :, :]),
                              "dX", 16)
                    wt(V, gld)
                    gsum = inc(V.tensor_scalar(out=scrD[:], in0=xt[par][:],
                                               scalar1=1.0, scalar2=0.0,
                                               op0=ALU.mult, op1=ALU.add,
                                               accum_out=sm1[par][:, 0:1]), "vSm")
                    wt(A, gld)
                    A.activation(out=scrA[:], in_=xt[par][:], func=AF.Square,
                                 accum_out=sq1[par][:, 0:1])
                    wt(A, gsum)
                    A.activation(out=mv[par][:], in_=sm1[par][:], func=AF.Copy,
                                 scale=1.0 / 2048.0)
                    A.activation(out=qv[par][:], in_=sq1[par][:], func=AF.Copy,
                                 scale=1.0 / 2048.0)
                    gm2 = inc(A.activation(out=m2[par][:], in_=mv[par][:],
                                           func=AF.Square), "aSm")
                    wt(V, gm2)
                    gvar = inc(V.scalar_tensor_tensor(
                        out=va[par][:], in0=qv[par][:], scalar=1e-5,
                        in1=m2[par][:], op0=ALU.add, op1=ALU.subtract), "vSm")
                    wt(A, gvar)
                    A.activation(out=lv[par][:], in_=va[par][:], func=AF.Ln)
                    grst = inc(A.activation(out=rst[par][:], in_=lv[par][:],
                                            func=AF.Exp, scale=-0.5), "aSm")
                    wt(V, grst)
                    V.tensor_scalar(out=xn2[par][:], in0=xt[par][:],
                                    scalar1=mv[par][:, 0:1],
                                    scalar2=rst[par][:, 0:1],
                                    op0=ALU.subtract, op1=ALU.mult)
                    V.tensor_reduce(out=am[:, :, None],
                                    in_=xn2[par][:].rearrange("p (b k) -> p b k", k=32),
                                    axis=AX.X, op=ALU.max, apply_absolute_value=True)
                    wt(V, rb_g[0])
                    grc = inc(V.reciprocal(out=rs2b[:], in_=am[:]), "vSm")
                    wt(A, grc)
                    gcp = inc(A.activation(out=rs2c[:], in_=rs2b[:], func=AF.Copy),
                              "aSm")
                    rb_g[0] = gcp
                    wt(V, gcp)
                    V.tensor_scalar(out=rs2[:], in0=rs2c[:], scalar1=224.0,
                                    scalar2=4.48e31, op0=ALU.mult, op1=ALU.min)
                    gsc = inc(V.tensor_scalar(out=ts2[:], in0=am[:],
                                              scalar1=1.0 / 224.0, scalar2=2e-30,
                                              op0=ALU.mult, op1=ALU.max), "vSm")
                    wt(A, gsc)
                    wt(A, f2f_g[0])
                    A.activation(out=rs2f[:], in_=rs2[:], func=AF.Copy)
                    A.activation(out=ts2f[:], in_=ts2[:], func=AF.Copy)
                    gfc = inc(A.activation(out=stl[:], in_=ts2f[:, 0:1],
                                           func=AF.Copy), "aSm")
                    wt(V, gfc)
                    V.tensor_tensor(out=q8[:].rearrange("p (b k) -> p b k", k=32),
                                    in0=xn2[par][:].rearrange("p (b k) -> p b k", k=32),
                                    in1=rs2f[:, :, None].broadcast_to([128, 64, 32]),
                                    op=ALU.mult)
                    wt(V, xq_g[par])
                    gxq = inc(V.tensor_tensor(
                        out=xq[par][:].rearrange("p (b k) -> p b k", k=32),
                        in0=q8[:].rearrange("p (b k) -> p b k", k=32),
                        in1=ts2f[:, :, None].broadcast_to([128, 64, 32]),
                        op=ALU.mult), "vXn")
                    f2f_g[0] = gxq
                    xt_g[par] = gxq
                    evg = None
                    for q in range(4):
                        pq_ = (blk * 4 + q) % 2
                        wt(P, gxq)
                        wt(P, ptr_g[pq_])
                        for u in range(4):
                            kd = q * 4 + u
                            mm = P.transpose(ptrF[pq_][:, u * 128:(u + 1) * 128],
                                             xq[par][:, kd * 128:(kd + 1) * 128],
                                             p_eye[:])
                        gtr = inc(mm, "mmT")
                        wt(A, gtr)
                        dst = xn2qT[:].rearrange("p (a t) -> p a t", t=T)[
                            :, q * 4:(q + 1) * 4, blk * 128:(blk + 1) * 128]
                        evg = inc(A.activation(out=dst, in_=ptrF[pq_][:].rearrange(
                            "p (a t) -> p a t", t=128), func=AF.Copy), "aTrE")
                        ptr_g[pq_] = evg
                    xq_g[par] = evg
                nc.all_engine_barrier()
                if debug_taps:
                    for di, (buf, w) in enumerate([
                            (am, 64), (rs2, 64), (ts2, 64), (xn2[1], D), (q8, D)]):
                        G.dma_start(out=dbg["pipe2"][di, :, 0:w],
                                    in_=buf[:, 0:w]).then_inc(sems["dS"], 16)
                        cnt["dS"] += 16
                    G.dma_start(out=dbg["xn2qT"][:, :], in_=xn2qT[:]).then_inc(
                        sems["dS"], 16)
                    cnt["dS"] += 16
                    wt(G, snap("dS"))
                    nc.all_engine_barrier()
            if phases < 7:
                return nc

            with ExitStack() as c7:
                e7 = c7.enter_context
                nat = [e7(nc.sbuf_tensor(f"nat{i}", [128, D], F32)) for i in range(2)]
                amw = e7(nc.sbuf_tensor("amw", [128, 64], F32))
                rsw = e7(nc.sbuf_tensor("rsw", [128, 64], F32))
                rswb = e7(nc.sbuf_tensor("rswb", [128, 64], F32))
                rswc = e7(nc.sbuf_tensor("rswc", [128, 64], F32))
                rswf = e7(nc.sbuf_tensor("rswf", [128, 64], F32))
                tswf = e7(nc.sbuf_tensor("tswf", [128, 64], F32))
                tsw = e7(nc.sbuf_tensor("tsw", [128, 64], F32))
                q8w = e7(nc.sbuf_tensor("q8w", [128, D], F8E4))
                wqb = e7(nc.sbuf_tensor("wqb", [128, D], BF16))
                wTg = [e7(nc.sbuf_tensor(f"wTg{i}", [128, KD * 512], BF16))
                       for i in range(2)]
                wTu = [e7(nc.sbuf_tensor(f"wTu{i}", [128, KD * 512], BF16))
                       for i in range(2)]
                sg = e7(nc.sbuf_tensor("sg", [128, 512], F32))
                hh4 = e7(nc.sbuf_tensor("hh4", [128, 2048], F32))
                amh = e7(nc.sbuf_tensor("amh", [128, 64], F32))
                rsh = e7(nc.sbuf_tensor("rsh", [128, 64], F32))
                rshb = e7(nc.sbuf_tensor("rshb", [128, 64], F32))
                rshc = e7(nc.sbuf_tensor("rshc", [128, 64], F32))
                rshf = e7(nc.sbuf_tensor("rshf", [128, 64], F32))
                tshf = e7(nc.sbuf_tensor("tshf", [128, 64], F32))
                tsh = e7(nc.sbuf_tensor("tsh", [128, 64], F32))
                q8h4 = e7(nc.sbuf_tensor("q8h4", [128, 2048], F8E4))
                hq4 = e7(nc.sbuf_tensor("hq4", [128, 2048], BF16))
                hqT = e7(nc.sbuf_tensor("hqT", [128, 4 * T], BF16))
                pg = [e7(nc.psum_tensor(f"pg{i}", [128, 512], F32)) for i in range(2)]
                pu = [e7(nc.psum_tensor(f"pu{i}", [128, 512], F32)) for i in range(2)]
                ptrG = [e7(nc.psum_tensor(f"ptrG{i}", [128, 512], BF16))
                        for i in range(2)]
                ptrH2 = [e7(nc.psum_tensor(f"ptrH2_{i}", [128, 512], BF16))
                         for i in range(2)]
                nat_g = [None, None]
                wT_ld = {}
                wT_g = [None, None]
                pg_g = [None, None]
                pu_g = [None, None]
                ptr_g = [None, None]
                ptr2_g = [None, None]
                hq_g = None
                hqT_g = None
                sg_g = None
                wqb_g = [None]
                rbw_g = [None]
                rbh_g = [None]
                gwf_g = [None]
                ghf_g = [None]
                hh4_g = None
                npiece = [0]

                def quant_piece(dst_wT, src_dram, ic, piece, dst_guard=None):
                    par = npiece[0] % 2
                    npiece[0] += 1
                    wt(SY, nat_g[par])
                    gld = inc(SY.dma_start(
                        out=nat[par][:],
                        in_=src_dram[(ic * 4 + piece) * 128:
                                     (ic * 4 + piece + 1) * 128, :]), "dW", 16)
                    wt(V, gld)
                    V.tensor_reduce(out=amw[:, :, None],
                                    in_=nat[par][:].rearrange("p (b k) -> p b k", k=32),
                                    axis=AX.X, op=ALU.max,
                                    apply_absolute_value=True)
                    wt(V, rbw_g[0])
                    grc = inc(V.reciprocal(out=rswb[:], in_=amw[:]), "vSm")
                    wt(A, grc)
                    gcp = inc(A.activation(out=rswc[:], in_=rswb[:], func=AF.Copy),
                              "aSm")
                    rbw_g[0] = gcp
                    wt(V, gcp)
                    V.tensor_scalar(out=rsw[:], in0=rswc[:], scalar1=224.0,
                                    scalar2=4.48e31, op0=ALU.mult, op1=ALU.min)
                    gsc = inc(V.tensor_scalar(out=tsw[:], in0=amw[:],
                                              scalar1=1.0 / 224.0, scalar2=2e-30,
                                              op0=ALU.mult, op1=ALU.max), "vSm")
                    wt(A, gsc)
                    wt(A, gwf_g[0])
                    A.activation(out=rswf[:], in_=rsw[:], func=AF.Copy)
                    A.activation(out=tswf[:], in_=tsw[:], func=AF.Copy)
                    gfc = inc(A.activation(out=stl[:], in_=tswf[:, 0:1],
                                           func=AF.Copy), "aSm")
                    wt(V, gfc)
                    V.tensor_tensor(out=q8w[:].rearrange("p (b k) -> p b k", k=32),
                                    in0=nat[par][:].rearrange("p (b k) -> p b k", k=32),
                                    in1=rswf[:, :, None].broadcast_to([128, 64, 32]),
                                    op=ALU.mult)
                    wt(V, wqb_g[0])
                    gwq = inc(V.tensor_tensor(
                        out=wqb[:].rearrange("p (b k) -> p b k", k=32),
                        in0=q8w[:].rearrange("p (b k) -> p b k", k=32),
                        in1=tswf[:, :, None].broadcast_to([128, 64, 32]),
                        op=ALU.mult), "vXn")
                    gwf_g[0] = gwq
                    nat_g[par] = gwq
                    evg = None
                    for q in range(4):
                        pq_ = q % 2
                        wt(P, gwq)
                        wt(P, ptr_g[pq_])
                        for u in range(4):
                            kd = q * 4 + u
                            mm = P.transpose(ptrG[pq_][:, u * 128:(u + 1) * 128],
                                             wqb[:, kd * 128:(kd + 1) * 128],
                                             p_eye[:])
                        gtr = inc(mm, "mmT")
                        if q == 3:
                            wqb_g[0] = gtr
                        wt(A, gtr)
                        if q == 0 and piece == 0:
                            wt(A, dst_guard)
                        dst = dst_wT[:].rearrange("p (a c) -> p a c", c=512)[
                            :, q * 4:(q + 1) * 4, piece * 128:(piece + 1) * 128]
                        evg = inc(A.activation(out=dst, in_=ptrG[pq_][:].rearrange(
                            "p (a t) -> p a t", t=128), func=AF.Copy), "aTrE")
                        ptr_g[pq_] = evg
                    return evg

                wT_use = [None, None]
                for piece in range(4):
                    wT_ld[("g", 0)] = quant_piece(wTg[0], w_fg.ap(), 0, piece)
                    wT_ld[("u", 0)] = quant_piece(wTu[0], w_fu.ap(), 0, piece)
                for ic in range(16):
                    icp = ic % 2
                    prep = []
                    if ic + 1 < 16:
                        for piece in range(4):
                            prep.append((wTg[(ic + 1) % 2], w_fg.ap(), piece, "g"))
                            prep.append((wTu[(ic + 1) % 2], w_fu.ap(), piece, "u"))
                    for i in range(16):
                        gp_ = (ic * 16 + i) % 2
                        wt(P, wT_ld[("g", ic)])
                        wt(P, pg_g[gp_])
                        for kd in range(KD):
                            mm = P.matmul(
                                pg[gp_][:],
                                xn2qT[:].rearrange("p (a t) -> p a t", t=T)[
                                    :, kd, i * 128:(i + 1) * 128],
                                wTg[icp][:].rearrange("p (a c) -> p a c", c=512)[
                                    :, kd, :],
                                start=(kd == 0), stop=(kd == KD - 1))
                        gmg = inc(mm, "mmG")
                        wt(P, wT_ld[("u", ic)])
                        wt(P, pu_g[gp_])
                        for kd in range(KD):
                            mm = P.matmul(
                                pu[gp_][:],
                                xn2qT[:].rearrange("p (a t) -> p a t", t=T)[
                                    :, kd, i * 128:(i + 1) * 128],
                                wTu[icp][:].rearrange("p (a c) -> p a c", c=512)[
                                    :, kd, :],
                                start=(kd == 0), stop=(kd == KD - 1))
                        gmu = inc(mm, "mmG")
                        wt(A, gmg)
                        wt(A, sg_g)
                        gsl = inc(A.activation(out=sg[:], in_=pg[gp_][:],
                                               func=AF.Silu), "aSig")
                        pg_g[gp_] = gsl
                        wt(V, gmu)
                        wt(V, gsl)
                        wt(V, hh4_g if i % 4 == 0 else None)
                        gh = inc(V.tensor_mul(hh4[:, (i % 4) * 512:(i % 4 + 1) * 512],
                                              sg[:], pu[gp_][:]), "vCmb")
                        pu_g[gp_] = gh
                        sg_g = gh
                        if i % 4 == 3:
                            V.tensor_reduce(out=amh[:, :, None],
                                            in_=hh4[:].rearrange("p (b k) -> p b k", k=32),
                                            axis=AX.X, op=ALU.max,
                                            apply_absolute_value=True)
                            wt(V, rbh_g[0])
                            grc = inc(V.reciprocal(out=rshb[:], in_=amh[:]), "vSm")
                            wt(A, grc)
                            gcp = inc(A.activation(out=rshc[:], in_=rshb[:],
                                                   func=AF.Copy), "aSm")
                            rbh_g[0] = gcp
                            wt(V, gcp)
                            V.tensor_scalar(out=rsh[:], in0=rshc[:], scalar1=224.0,
                                            scalar2=4.48e31, op0=ALU.mult,
                                            op1=ALU.min)
                            gsc = inc(V.tensor_scalar(out=tsh[:], in0=amh[:],
                                                      scalar1=1.0 / 224.0,
                                                      scalar2=2e-30, op0=ALU.mult,
                                                      op1=ALU.max), "vSm")
                            wt(A, gsc)
                            wt(A, ghf_g[0])
                            A.activation(out=rshf[:], in_=rsh[:], func=AF.Copy)
                            A.activation(out=tshf[:], in_=tsh[:], func=AF.Copy)
                            gfc = inc(A.activation(out=stl[:], in_=tshf[:, 0:1],
                                                   func=AF.Copy), "aSm")
                            wt(V, gfc)
                            V.tensor_tensor(
                                out=q8h4[:].rearrange("p (b k) -> p b k", k=32),
                                in0=hh4[:].rearrange("p (b k) -> p b k", k=32),
                                in1=rshf[:, :, None].broadcast_to([128, 64, 32]),
                                op=ALU.mult)
                            wt(V, hq_g)
                            ghq = inc(V.tensor_tensor(
                                out=hq4[:].rearrange("p (b k) -> p b k", k=32),
                                in0=q8h4[:].rearrange("p (b k) -> p b k", k=32),
                                in1=tshf[:, :, None].broadcast_to([128, 64, 32]),
                                op=ALU.mult), "vXb")
                            ghf_g[0] = ghq
                            hh4_g = ghq
                            for ir in range(4):
                                i0 = i - 3 + ir
                                pq_ = ir % 2
                                wt(P, ghq)
                                wt(P, ptr2_g[pq_])
                                for u in range(4):
                                    mm = P.transpose(
                                        ptrH2[pq_][:, u * 128:(u + 1) * 128],
                                        hq4[:, ir * 512 + u * 128:
                                            ir * 512 + (u + 1) * 128], p_eye[:])
                                gtr = inc(mm, "mmT")
                                hq_g = gtr
                                wt(A, gtr)
                                wt(A, hqT_g)
                                dst = hqT[:].rearrange("p (a t) -> p a t", t=T)[
                                    :, :, i0 * 128:(i0 + 1) * 128]
                                evg = inc(A.activation(
                                    out=dst, in_=ptrH2[pq_][:].rearrange(
                                        "p (a t) -> p a t", t=128),
                                    func=AF.Copy), "aTrE")
                                ptr2_g[pq_] = evg
                        if INTERLEAVE_PREP and i >= 8 and prep:
                            dstw, srcw, piece, tag = prep.pop(0)
                            gq = quant_piece(dstw, srcw, ic + 1, piece,
                                             dst_guard=wT_use[(ic + 1) % 2])
                            wT_ld[(tag, ic + 1)] = gq
                    if not INTERLEAVE_PREP:
                        while prep:
                            dstw, srcw, piece, tag = prep.pop(0)
                            gq = quant_piece(dstw, srcw, ic + 1, piece,
                                             dst_guard=wT_use[(ic + 1) % 2])
                            wT_ld[(tag, ic + 1)] = gq
                    wT_use[icp] = snap("mmG")
                    wt(G, snap("aTrE"))
                    for u in range(4):
                        hqT_g = inc(G.dma_start(out=hqT_d[ic * 4 + u, :, :],
                                                in_=hqT[:, u * T:(u + 1) * T]),
                                    "dS", 16)
                wt(SY, snap("dS"))
                nc.all_engine_barrier()
                if debug_taps:
                    for di, (buf, w) in enumerate([(hh4, 2048), (q8h4, 2048),
                                                   (hq4, 2048), (sg, 512),
                                                   (tsh, 64)]):
                        G.dma_start(out=dbg["pipe2"][di, :, 0:w],
                                    in_=buf[:, 0:w]).then_inc(sems["dS"], 16)
                        cnt["dS"] += 16
                    for sI in range(I // 128):
                        G.dma_start(out=dbg["hqT"][sI, :, :],
                                    in_=hqT_d[sI, :, :]).then_inc(sems["dS"], 16)
                        cnt["dS"] += 16
                    wt(G, snap("dS"))
                    nc.all_engine_barrier()
        if phases < 8:
            return nc

        with ExitStack() as c8:
            e8 = c8.enter_context
            natd = [e8(nc.sbuf_tensor(f"natd{i}", [128, 2048], F32)) for i in range(2)]
            amd = e8(nc.sbuf_tensor("amd", [128, 64], F32))
            rsd = e8(nc.sbuf_tensor("rsd", [128, 64], F32))
            rsdb = e8(nc.sbuf_tensor("rsdb", [128, 64], F32))
            rsdc = e8(nc.sbuf_tensor("rsdc", [128, 64], F32))
            rsdf = e8(nc.sbuf_tensor("rsdf", [128, 64], F32))
            tsdf = e8(nc.sbuf_tensor("tsdf", [128, 64], F32))
            tsd = e8(nc.sbuf_tensor("tsd", [128, 64], F32))
            q8d = e8(nc.sbuf_tensor("q8d", [128, 2048], F8E4))
            wqd = e8(nc.sbuf_tensor("wqd", [128, 2048], BF16))
            wTd = e8(nc.sbuf_tensor("wTd", [128, 64 * 512], BF16))
            hqt = [e8(nc.sbuf_tensor(f"hqt{i}", [128, 64 * 128], BF16))
                   for i in range(2)]
            x2t = [e8(nc.sbuf_tensor(f"x2t{i}", [128, 512], F32)) for i in range(2)]
            of = [e8(nc.sbuf_tensor(f"of{i}", [128, 512], F32)) for i in range(2)]
            pd = [e8(nc.psum_tensor(f"pd{i}", [128, 512], F32)) for i in range(2)]
            ptrH = [e8(nc.psum_tensor(f"ptrH{i}", [128, 512], BF16)) for i in range(2)]
            natd_g = [None, None]
            hqt_g = [None, None]
            x2t_g = [None, None]
            of_g = [None, None]
            pd_g = [None, None]
            ptr_g = [None, None]
            wTd_g = None
            wqd_g = [None]
            rbd_g = [None]
            ghd_g = [None]
            for eh in range(4):
                for pc in range(16):
                    eblk = pc // 4
                    ipc = pc % 4
                    par = pc % 2
                    erow = eh * 512 + eblk * 128
                    wt(SY, natd_g[par])
                    gld = inc(SY.dma_start(
                        out=natd[par][:],
                        in_=w_fd[erow:erow + 128, ipc * 2048:(ipc + 1) * 2048]),
                        "dW", 16)
                    wt(V, gld)
                    V.tensor_reduce(out=amd[:, :, None],
                                    in_=natd[par][:].rearrange("p (b k) -> p b k", k=32),
                                    axis=AX.X, op=ALU.max,
                                    apply_absolute_value=True)
                    wt(V, rbd_g[0])
                    grc = inc(V.reciprocal(out=rsdb[:], in_=amd[:]), "vSm")
                    wt(A, grc)
                    gcp = inc(A.activation(out=rsdc[:], in_=rsdb[:], func=AF.Copy),
                              "aSm")
                    rbd_g[0] = gcp
                    wt(V, gcp)
                    V.tensor_scalar(out=rsd[:], in0=rsdc[:], scalar1=224.0,
                                    scalar2=4.48e31, op0=ALU.mult, op1=ALU.min)
                    gsc = inc(V.tensor_scalar(out=tsd[:], in0=amd[:],
                                              scalar1=1.0 / 224.0, scalar2=2e-30,
                                              op0=ALU.mult, op1=ALU.max), "vSm")
                    wt(A, gsc)
                    wt(A, ghd_g[0])
                    A.activation(out=rsdf[:], in_=rsd[:], func=AF.Copy)
                    A.activation(out=tsdf[:], in_=tsd[:], func=AF.Copy)
                    gfc = inc(A.activation(out=stl[:], in_=tsdf[:, 0:1],
                                           func=AF.Copy), "aSm")
                    wt(V, gfc)
                    V.tensor_tensor(out=q8d[:].rearrange("p (b k) -> p b k", k=32),
                                    in0=natd[par][:].rearrange("p (b k) -> p b k", k=32),
                                    in1=rsdf[:, :, None].broadcast_to([128, 64, 32]),
                                    op=ALU.mult)
                    wt(V, wqd_g[0])
                    gwq = inc(V.tensor_tensor(
                        out=wqd[:].rearrange("p (b k) -> p b k", k=32),
                        in0=q8d[:].rearrange("p (b k) -> p b k", k=32),
                        in1=tsdf[:, :, None].broadcast_to([128, 64, 32]),
                        op=ALU.mult), "vXn")
                    ghd_g[0] = gwq
                    natd_g[par] = gwq
                    for q in range(4):
                        pq_ = q % 2
                        wt(P, gwq)
                        wt(P, ptr_g[pq_])
                        for u in range(4):
                            kd = q * 4 + u
                            mm = P.transpose(ptrH[pq_][:, u * 128:(u + 1) * 128],
                                             wqd[:, kd * 128:(kd + 1) * 128],
                                             p_eye[:])
                        gtr = inc(mm, "mmT")
                        if q == 3:
                            wqd_g[0] = gtr
                        wt(A, gtr)
                        if pc == 0 and q == 0:
                            wt(A, wTd_g)
                        isub = ipc * 16 + q * 4
                        dst = wTd[:].rearrange("p (a e) -> p a e", e=512)[
                            :, isub:isub + 4, eblk * 128:(eblk + 1) * 128]
                        evg = inc(A.activation(out=dst, in_=ptrH[pq_][:].rearrange(
                            "p (a t) -> p a t", t=128), func=AF.Copy), "aTrE")
                        ptr_g[pq_] = evg
                wprep = snap("aTrE")
                for i in range(16):
                    par = i % 2
                    wt(SY, hqt_g[par])
                    ghl = inc(SY.dma_start(
                        out=hqt[par][:].rearrange("p (a t) -> p a t", t=128),
                        in_=hqT_d.ap().rearrange("a p t -> p a t")[
                            :, :, i * 128:(i + 1) * 128]), "dX", 16)
                    wt(SY, x2t_g[par])
                    gxl = inc(SY.dma_start(
                        out=x2t[par][:],
                        in_=x2_d[i, :, eh * 512:(eh + 1) * 512]), "dX", 16)
                    wt(P, ghl)
                    wt(P, wprep)
                    wt(P, pd_g[par])
                    for sI in range(64):
                        mm = P.matmul(
                            pd[par][:],
                            hqt[par][:, sI * 128:(sI + 1) * 128],
                            wTd[:].rearrange("p (a e) -> p a e", e=512)[:, sI, :],
                            start=(sI == 0), stop=(sI == 63))
                    gmd = inc(mm, "mmD")
                    hqt_g[par] = gmd
                    wt(V, gmd)
                    wt(V, gxl)
                    wt(V, of_g[par])
                    gev = inc(V.tensor_add(of[par][:], pd[par][:], x2t[par][:]),
                              "vEvO")
                    pd_g[par] = gev
                    x2t_g[par] = gev
                    wt(G, gev)
                    of_g[par] = inc(G.dma_start(
                        out=out[i * 128:(i + 1) * 128, eh * 512:(eh + 1) * 512],
                        in_=of[par][:]), "oDone", 16)
                wTd_g = snap("mmD")
            wt(G, snap("oDone"))

    return nc


def _layout_params(ins):
    f = np.float32
    cw = np.asarray(ins["L0_attn_conv_weight"], f)
    cb = np.asarray(ins["L0_attn_conv_bias"], f)
    dtw = np.asarray(ins["L0_attn_dt_w"], f)
    dtb = np.asarray(ins["L0_attn_dt_b"], f)
    al = np.asarray(ins["L0_attn_A_log"], f)
    B = np.asarray(ins["L0_attn_B"], f)
    Cc = np.asarray(ins["L0_attn_C"], f)
    Dc = np.asarray(ins["L0_attn_D"], f)

    def chanv(v):
        return np.ascontiguousarray(v.reshape(32, 128).T)

    cw4 = np.ascontiguousarray(
        cw.reshape(32, 128, 4).transpose(1, 0, 2).reshape(128, 128))
    dtwf = dtw.reshape(C)
    dt2 = np.zeros((128, 32 * 128), f)
    for s in range(32):
        col = dtwf[s * 128:(s + 1) * 128]
        blk = np.zeros((128, 128), f)
        blk[0:64, 0:64] = col[0:64, None]
        blk[64:128, 64:128] = col[64:128, None]
        dt2[:, s * 128:(s + 1) * 128] = blk
    return dict(
        cw4=cw4, cbv=chanv(cb), dt2=dt2, dtbc=chanv(np.repeat(dtb, 64)),
        alog=chanv(al.reshape(C)), bcv=chanv(B.reshape(C)),
        ccv=chanv(Cc.reshape(C)), dcv=chanv(Dc.reshape(C)),
        eye=np.eye(128, dtype=f),
    )


_NC_CACHE = {}
_LAST_RES = None


def kernel(**ins):
    x = np.asarray(ins["L0_attn_hidden_states"], np.float32)
    f = np.float32
    weights = dict(
        w_inT=np.ascontiguousarray(np.asarray(ins["L0_attn_in_proj_w"], f).T),
        w_gT=np.ascontiguousarray(np.asarray(ins["L0_attn_gate_proj_w"], f).T),
        w_oT=np.ascontiguousarray(np.asarray(ins["L0_attn_out_proj_w"], f).T),
        w_fg=np.ascontiguousarray(np.asarray(ins["L0_ffn_gate_w"], f)),
        w_fu=np.ascontiguousarray(np.asarray(ins["L0_ffn_up_w"], f)),
        w_fd=np.ascontiguousarray(np.asarray(ins["L0_ffn_down_w"], f)),
    )
    params = _layout_params(ins)

    key = ("prog", DEBUG_TAPS, PHASES, MIXK, MIXSTAGE)
    if key not in _NC_CACHE:
        _NC_CACHE[key] = build_program(DEBUG_TAPS, PHASES, MIXK, MIXSTAGE)
    nc = _NC_CACHE[key]

    in_maps = []
    for core in range(8):
        b, half = core // 2, core % 2
        seq = x[b]
        hid_a = np.zeros((TH, D), f)
        lo = half * 2048 - 128
        hid_a[max(0, -lo):, :] = seq[max(lo, 0): half * 2048 + 2048, :]
        pre_a = np.zeros((TH, D), f)
        if half == 1:
            pre_a[128:, :] = seq[0:2048, :]
        m = dict(hid=np.ascontiguousarray(hid_a), pre=np.ascontiguousarray(pre_a),
                 smask=np.full((128, 1), float(half), f))
        m.update(weights)
        m.update(params)
        in_maps.append(m)

    res = run_bass_kernel_spmd(nc, in_maps, core_ids=list(range(8)))
    global _LAST_RES
    _LAST_RES = res
    outp = np.empty((4, 4096, 2048), np.float32)
    for core in range(8):
        b, half = core // 2, core % 2
        outp[b, half * 2048:(half + 1) * 2048, :] = res.results[core]["out"]
    return outp


if __name__ == "__main__":
    t_nc = build_program(DEBUG_TAPS)
    print("built:", t_nc.n_instructions(), "instructions")


# version 04
# speedup vs baseline: 1.0142x; 1.0142x over previous
import numpy as np
import concourse.bass as bass
import concourse.mybir as mybir
from concourse.bass_utils import run_bass_kernel_spmd
from contextlib import ExitStack

F32 = mybir.dt.float32
BF16 = mybir.dt.bfloat16
F8E4 = mybir.dt.float8e4
AF = mybir.ActivationFunctionType
ALU = mybir.AluOpType
AX = mybir.AxisListType

D = 2048
C = 4096
I = 8192
T = 2048
TH = 2176
KD = D // 128
NSUB = C // 128
CCH = 128
NCH = C // CCH
JSUB = CCH // 128
NBLK = TH // 128

DEBUG_TAPS = False
PHASES = 99
MIXK = 99
MIXSTAGE = 99
INTERLEAVE_PREP = True


def build_program(debug_taps=False, phases=99, mixk=99, mixstage=99):
    nc = bass.Bass("TRN2", target_bir_lowering=False,
                   detect_race_conditions=False)

    hid = nc.dram_tensor("hid", [TH, D], F32, kind="ExternalInput")
    pre = nc.dram_tensor("pre", [TH, D], F32, kind="ExternalInput")
    smask = nc.dram_tensor("smask", [128, 1], F32, kind="ExternalInput")
    w_inT = nc.dram_tensor("w_inT", [D, C], F32, kind="ExternalInput")
    w_gT = nc.dram_tensor("w_gT", [D, C], F32, kind="ExternalInput")
    w_oT = nc.dram_tensor("w_oT", [C, D], F32, kind="ExternalInput")
    w_fg = nc.dram_tensor("w_fg", [I, D], F32, kind="ExternalInput")
    w_fu = nc.dram_tensor("w_fu", [I, D], F32, kind="ExternalInput")
    w_fd = nc.dram_tensor("w_fd", [D, I], F32, kind="ExternalInput")
    cw4 = nc.dram_tensor("cw4", [128, 128], F32, kind="ExternalInput")
    cbv = nc.dram_tensor("cbv", [128, 32], F32, kind="ExternalInput")
    dt2 = nc.dram_tensor("dt2", [128, 32 * 128], F32, kind="ExternalInput")
    dtbc = nc.dram_tensor("dtbc", [128, 32], F32, kind="ExternalInput")
    alog = nc.dram_tensor("alog", [128, 32], F32, kind="ExternalInput")
    bcv = nc.dram_tensor("bcv", [128, 32], F32, kind="ExternalInput")
    ccv = nc.dram_tensor("ccv", [128, 32], F32, kind="ExternalInput")
    dcv = nc.dram_tensor("dcv", [128, 32], F32, kind="ExternalInput")
    eye = nc.dram_tensor("eye", [128, 128], F32, kind="ExternalInput")

    out = nc.dram_tensor("out", [T, D], F32, kind="ExternalOutput")

    comb_d = nc.dram_tensor("comb_d", [NSUB, 128, T], BF16)
    x2_d = nc.dram_tensor("x2_d", [16, 128, D], F32)
    hqT_d = nc.dram_tensor("hqT_d", [I // 128, 128, T], BF16)

    dbg = {}
    if debug_taps:
        dbg["xnT"] = nc.dram_tensor("dbg_xnT", [128, KD * TH], F32, kind="ExternalOutput")
        dbg["spre"] = nc.dram_tensor("dbg_spre", [128, 32], F32, kind="ExternalOutput")
        dbg["comb"] = nc.dram_tensor("dbg_comb", [NSUB, 128, T], F32, kind="ExternalOutput")
        dbg["x2"] = nc.dram_tensor("dbg_x2", [16, 128, D], F32, kind="ExternalOutput")
        dbg["hqT"] = nc.dram_tensor("dbg_hqT", [I // 128, 128, T], F32, kind="ExternalOutput")
        dbg["xn2qT"] = nc.dram_tensor("dbg_xn2qT", [128, KD * T], F32, kind="ExternalOutput")
        dbg["pipe"] = nc.dram_tensor("dbg_pipe", [7, 128, TH], F32, kind="ExternalOutput")
        dbg["pipe2"] = nc.dram_tensor("dbg_pipe2", [5, 128, D], F32, kind="ExternalOutput")

    ctx = ExitStack()
    with ctx:
        e = ctx.enter_context

        p_cw4 = e(nc.sbuf_tensor("p_cw4", [128, 128], F32))
        p_cb = e(nc.sbuf_tensor("p_cb", [128, 32], F32))
        p_dt2 = e(nc.sbuf_tensor("p_dt2", [128, 32 * 128], BF16))
        p_dtbc = e(nc.sbuf_tensor("p_dtbc", [128, 32], F32))
        p_ac = e(nc.sbuf_tensor("p_ac", [128, 32], F32))
        p_bc = e(nc.sbuf_tensor("p_bc", [128, 32], F32))
        p_cc = e(nc.sbuf_tensor("p_cc", [128, 32], F32))
        p_dc = e(nc.sbuf_tensor("p_dc", [128, 32], F32))
        p_sm = e(nc.sbuf_tensor("p_sm", [128, 1], F32))
        p_eye = e(nc.sbuf_tensor("p_eye", [128, 128], BF16))
        spre = e(nc.sbuf_tensor("spre", [128, 32], F32))
        s0 = e(nc.sbuf_tensor("s0", [128, 32], F32))
        zcol = e(nc.sbuf_tensor("zcol", [128, 1], F32))
        stl = e(nc.sbuf_tensor("stl", [128, 1], F32))

        sems = {}
        for name in ["dPrm", "dPrmG", "dX", "dW", "dWG", "dS", "mmP", "mmG", "mmD", "mmT",
                     "aEv", "aSil", "aSpA", "aA", "aSig", "aSm", "aTrE",
                     "vXb", "vSpD", "vDtb", "vScan", "vCmb", "vUT", "vXn",
                     "vSm", "vEvO", "pCv", "oDone", "dCmb"]:
            sems[name] = e(nc.semaphore(name))
        cnt = dict.fromkeys(sems, 0)

        def inc(instr, name, k=1):
            instr.then_inc(sems[name], k)
            cnt[name] += k
            return (name, cnt[name])

        def snap(name):
            return (name, cnt[name])

        def wt(eng, guard):
            if guard is None:
                return
            name, v = guard
            if v > 0:
                eng.wait_ge(sems[name], v)

        V, A, P, G, SY = nc.vector, nc.scalar, nc.tensor, nc.gpsimd, nc.sync

        gp0 = None
        for (dst, src) in [(p_cw4, cw4), (p_cb, cbv), (p_dtbc, dtbc),
                           (p_ac, alog), (p_bc, bcv), (p_cc, ccv),
                           (p_dc, dcv), (p_sm, smask)]:
            gp0 = inc(SY.dma_start(out=dst[:], in_=src[:, :]), "dPrm", 16)
        gp1 = inc(G.dma_start(out=p_dt2[:], in_=dt2[:, :]), "dPrmG", 16)
        gp1 = inc(G.dma_start(out=p_eye[:], in_=eye[:, :]), "dPrmG", 16)
        for eng in (A, V, P, G):
            wt(eng, gp0)
            wt(eng, gp1)
        A.activation(out=spre[:], in_=p_ac[:], func=AF.Exp)
        gac = inc(A.activation(out=p_ac[:], in_=spre[:], func=AF.Copy,
                               scale=-1.0), "aSm")
        V.memset(zcol[:], 0.0)
        nc.all_engine_barrier()

        def emit_xnT(src, xnT, ptr, nblk, lnbufs):
            (xt, scrD, scrA, sm1, sq1, mv, qv, m2, va, lv, rs, xnb) = lnbufs
            if True:
                xt_g = [None, None]
                xnb_g = [None, None]
                ptr_g = [None, None]
                TW = xnT.ap().shape[1] // KD
                for blk in range(nblk):
                    par = blk % 2
                    wt(SY, xt_g[par])
                    gld = inc(SY.dma_start(out=xt[par][:],
                                           in_=src[blk * 128:(blk + 1) * 128, :]),
                              "dX", 16)
                    wt(V, gld)
                    gsum = inc(V.tensor_scalar(out=scrD[:], in0=xt[par][:],
                                               scalar1=1.0, scalar2=0.0,
                                               op0=ALU.mult, op1=ALU.add,
                                               accum_out=sm1[par][:, 0:1]), "vSm")
                    wt(A, gld)
                    A.activation(out=scrA[:], in_=xt[par][:], func=AF.Square,
                                 accum_out=sq1[par][:, 0:1])
                    wt(A, gsum)
                    A.activation(out=mv[par][:], in_=sm1[par][:], func=AF.Copy,
                                 scale=1.0 / 2048.0)
                    A.activation(out=qv[par][:], in_=sq1[par][:], func=AF.Copy,
                                 scale=1.0 / 2048.0)
                    gm2 = inc(A.activation(out=m2[par][:], in_=mv[par][:],
                                           func=AF.Square), "aSm")
                    wt(V, gm2)
                    gvar = inc(V.scalar_tensor_tensor(
                        out=va[par][:], in0=qv[par][:], scalar=1e-5,
                        in1=m2[par][:], op0=ALU.add, op1=ALU.subtract), "vSm")
                    wt(A, gvar)
                    A.activation(out=lv[par][:], in_=va[par][:], func=AF.Ln)
                    grst = inc(A.activation(out=rs[par][:], in_=lv[par][:],
                                            func=AF.Exp, scale=-0.5), "aSm")
                    wt(V, grst)
                    wt(V, xnb_g[par])
                    gxn = inc(V.tensor_scalar(out=xnb[par][:], in0=xt[par][:],
                                              scalar1=mv[par][:, 0:1],
                                              scalar2=rs[par][:, 0:1],
                                              op0=ALU.subtract, op1=ALU.mult), "vXn")
                    xt_g[par] = gxn
                    evg = None
                    for q in range(4):
                        pq_ = (blk * 4 + q) % 2
                        wt(P, gxn)
                        wt(P, ptr_g[pq_])
                        for u in range(4):
                            k = q * 4 + u
                            mm = P.transpose(ptr[pq_][:, u * 128:(u + 1) * 128],
                                             xnb[par][:, k * 128:(k + 1) * 128],
                                             p_eye[:])
                        gtr = inc(mm, "mmT")
                        wt(A, gtr)
                        dst = xnT[:].rearrange("p (a t) -> p a t", t=TW)[
                            :, q * 4:(q + 1) * 4, blk * 128:(blk + 1) * 128]
                        evg = inc(A.activation(out=dst, in_=ptr[pq_][:].rearrange(
                            "p (a t) -> p a t", t=128), func=AF.Copy), "aTrE")
                        ptr_g[pq_] = evg
                    xnb_g[par] = evg

        def emit_mixer(xnT, prefix, bufs):
            (winT, wgT, pp, pq, pdt, proj_buf, cA, cB, cX, xtile, xtile_b,
             spM, spL, spN, dtb_sb, a_buf, u_buf, s_sb, g_sb, t1_sb, t2_sb,
             comb_sb) = bufs
            win_ld = [None] * NCH
            wg_ld = [None] * NCH
            win_g = [None, None]
            wg_g = [None, None]
            pp_g = [None, None]
            pq_g = [None, None]
            pdt_g = [[], []]
            pb_g = [None, None]
            cX_g = [None, None]
            xt_g = [None, None]
            spM_g = [None, None]
            spL_g = [None, None]
            dtb_g = [None, None]
            a_g = [None, None]
            cmb_g = [None, None]

            srcv = w_inT.ap().rearrange("(a p) c -> p a c", p=128)
            srcg = w_gT.ap().rearrange("(a p) c -> p a c", p=128)
            pend = [None]
            pend_b = [None]
            xb_g = [None, None]

            def emit_stageB(pb_):
                gp, s, tt, xb_rdy = pb_
                if mixstage < 4:
                    return
                wt(P, xb_rdy)
                for gg in pdt_g[gp]:
                    wt(P, gg)
                gmd = inc(P.matmul(pdt[gp][:],
                                   p_dt2[:, s * 128:(s + 1) * 128],
                                   xtile_b[gp][:], start=True,
                                   stop=True), "mmD")
                xb_g[gp] = gmd
                wt(V, gmd)
                wt(V, spM_g[gp])
                gsp1 = inc(V.tensor_scalar(out=spM[gp][:], in0=pdt[gp][:],
                                           scalar1=p_dtbc[:, s:s + 1],
                                           scalar2=0.0, op0=ALU.add,
                                           op1=ALU.max), "vSpD")
                wt(A, gmd)
                wt(A, gsp1)
                wt(A, spL_g[gp])
                A.activation(out=spL[gp][:], in_=pdt[gp][:], func=AF.Abs,
                             bias=p_dtbc[:, s:s + 1])
                A.activation(out=spN[gp][:], in_=spL[gp][:], func=AF.Exp,
                             scale=-1.0)
                gsp4 = inc(A.activation(out=spL[gp][:], in_=spN[gp][:],
                                        func=AF.Ln, bias=1.0), "aSpA")
                pdt_g[gp] = [gsp4, gsp1]
                wt(V, gsp4)
                wt(V, dtb_g[gp])
                gdtb = inc(V.tensor_add(dtb_sb[gp][:], spM[gp][:],
                                        spL[gp][:]), "vDtb")
                spM_g[gp] = gdtb
                spL_g[gp] = gdtb
                if mixstage < 5:
                    return
                wt(A, gdtb)
                wt(A, a_g[gp])
                ga = inc(A.activation(out=a_buf[gp][:, 1:513],
                                      in_=dtb_sb[gp][:], func=AF.Exp,
                                      scale=p_ac[:, s:s + 1]), "aA")
                dtb_g[gp] = ga
                if mixstage < 6:
                    return
                gu = V.scalar_tensor_tensor(out=u_buf[gp][:, 1:513],
                                            in0=dtb_sb[gp][:],
                                            scalar=p_bc[:, s:s + 1],
                                            in1=xtile[gp][:],
                                            op0=ALU.mult, op1=ALU.mult)
                if prefix:
                    xt_g[gp] = inc(gu, "vUT")
                g = sB_ng[0]
                if tt == 1:
                    src_c = zcol[:, 0:1] if prefix else s0[:, s:s + 1]
                    V.tensor_copy(out=u_buf[gp][:, 0:1], in_=src_c)
                else:
                    V.tensor_copy(out=u_buf[gp][:, 0:1],
                                  in_=s_sb[(g - 1) % 2][:, 512:513])
                wt(V, ga)
                gsc = inc(V.tensor_tensor_scan(out=s_sb[gp][:],
                                               data0=a_buf[gp][:],
                                               data1=u_buf[gp][:],
                                               initial=0.0, op0=ALU.mult,
                                               op1=ALU.add), "vScan")
                a_g[gp] = gsc
                if prefix:
                    if tt == 4:
                        wt(G, gsc)
                        G.tensor_copy(out=spre[:, s:s + 1],
                                      in_=s_sb[gp][:, 512:513])
                else:
                    wt(A, gmg_q[g])
                    gsig = inc(A.activation(out=g_sb[gp][:],
                                            in_=pq[gp][:],
                                            func=AF.Sigmoid), "aSig")
                    pq_g[gp] = gsig
                    if pend[0] is not None:
                        emit_ystage(pend[0])
                    pend[0] = (gp, s, tt, gsig)
                sB_ng[0] += 1

            sB_ng = [0]
            gmg_q = {}

            def emit_ystage(p_):
                gp_, s_, tt_, gsig_ = p_
                V.tensor_scalar(out=t1_sb[gp_][:], in0=xtile[gp_][:],
                                scalar1=p_dc[:, s_:s_ + 1],
                                scalar2=None, op0=ALU.mult)
                gut = inc(V.scalar_tensor_tensor(
                    out=t2_sb[gp_][:], in0=s_sb[gp_][:, 1:513],
                    scalar=p_cc[:, s_:s_ + 1], in1=t1_sb[gp_][:],
                    op0=ALU.mult, op1=ALU.add), "vUT")
                xt_g[gp_] = gut
                wt(V, gsig_)
                wt(V, cmb_g[gp_])
                gcmb = inc(V.tensor_tensor(out=comb_sb[gp_][:],
                                           in0=t2_sb[gp_][:],
                                           in1=g_sb[gp_][:],
                                           op=ALU.mult), "vCmb")
                wt(G, gcmb)
                cmb_g[gp_] = inc(G.dma_start(
                    out=comb_d[s_, :, (tt_ - 1) * 512:tt_ * 512],
                    in_=comb_sb[gp_][:]), "dCmb", 16)

            def load_chunk(k):
                kb = k % 2
                wt(G, win_g[kb])
                win_ld[k] = inc(G.dma_start(
                    out=winT[kb][:].rearrange("p (a c) -> p a c", c=CCH),
                    in_=srcv[:, :, k * CCH:(k + 1) * CCH]), "dWG", 16)
                if not prefix:
                    wt(G, wg_g[kb])
                    wg_ld[k] = inc(G.dma_start(
                        out=wgT[kb][:].rearrange("p (a c) -> p a c", c=CCH),
                        in_=srcg[:, :, k * CCH:(k + 1) * CCH]), "dWG", 16)

            load_chunk(0)
            load_chunk(1)
            ng = 0
            nip = 0
            for k in range(min(NCH, mixk)):
                kb = k % 2
                for j in range(JSUB):
                    s = k * JSUB + j
                    pbs = proj_buf[s % 2]
                    for tt in range(5):
                        par = nip % 2
                        t0 = 0 if tt == 0 else 128 + (tt - 1) * 512
                        tw = 128 if tt == 0 else 512
                        gp = ng % 2
                        wt(P, win_ld[k])
                        wt(P, pp_g[par])
                        for kd in range(KD):
                            mm = P.matmul(
                                pp[par][:, 0:tw],
                                winT[kb][:].rearrange("p (a c) -> p a c", c=CCH)[
                                    :, kd, j * 128:(j + 1) * 128],
                                xnT[:].rearrange("p (a t) -> p a t", t=TH)[
                                    :, kd, t0:t0 + tw],
                                start=(kd == 0), stop=(kd == KD - 1))
                        gmp = inc(mm, "mmP")
                        if (not prefix) and tt > 0:
                            wt(P, wg_ld[k])
                            wt(P, pq_g[gp])
                            for kd in range(KD):
                                mm = P.matmul(
                                    pq[gp][:],
                                    wgT[kb][:].rearrange("p (a c) -> p a c", c=CCH)[
                                        :, kd, j * 128:(j + 1) * 128],
                                    xnT[:].rearrange("p (a t) -> p a t", t=TH)[
                                        :, kd, t0:t0 + tw],
                                    start=(kd == 0), stop=(kd == KD - 1))
                            gmg_q[ng] = inc(mm, "mmG")
                        wt(A, gmp)
                        wt(A, pb_g[s % 2] if tt == 0 else None)
                        gev = inc(A.activation(out=pbs[:, t0:t0 + tw],
                                               in_=pp[par][:, 0:tw], func=AF.Copy),
                                  "aEv")
                        pp_g[par] = gev
                        if pend_b[0] is not None:
                            emit_stageB(pend_b[0])
                            pend_b[0] = None
                        if tt > 0 and mixstage >= 2:
                            gp = ng % 2
                            wt(V, gev)
                            wt(V, cX_g[gp])
                            o = t0 - 3
                            V.tensor_scalar(out=cA[:], in0=pbs[:, o:o + 512],
                                            scalar1=p_cw4[:, s * 4:s * 4 + 1],
                                            scalar2=p_cb[:, s:s + 1],
                                            op0=ALU.mult, op1=ALU.add)
                            V.scalar_tensor_tensor(out=cB[:], in0=pbs[:, o + 1:o + 513],
                                                   scalar=p_cw4[:, s * 4 + 1:s * 4 + 2],
                                                   in1=cA[:], op0=ALU.mult, op1=ALU.add)
                            V.scalar_tensor_tensor(out=cA[:], in0=pbs[:, o + 2:o + 514],
                                                   scalar=p_cw4[:, s * 4 + 2:s * 4 + 3],
                                                   in1=cB[:], op0=ALU.mult, op1=ALU.add)
                            gcv = inc(V.scalar_tensor_tensor(
                                out=cX[gp][:], in0=pbs[:, o + 3:o + 515],
                                scalar=p_cw4[:, s * 4 + 3:s * 4 + 4],
                                in1=cA[:], op0=ALU.mult, op1=ALU.add), "pCv")
                            if tt == 4:
                                pb_g[s % 2] = gcv
                        if tt > 0 and mixstage >= 3:
                            gp = ng % 2
                            wt(A, gcv)
                            wt(A, xt_g[gp])
                            gsl = inc(A.activation(out=xtile[gp][:], in_=cX[gp][:],
                                                   func=AF.Silu), "aSil")
                            cX_g[gp] = gsl
                            wt(V, gsl)
                            wt(V, xb_g[gp])
                            xb_rdy = inc(V.tensor_copy(out=xtile_b[gp][:],
                                                       in_=xtile[gp][:]), "vXb")
                            pend_b[0] = (gp, s, tt, xb_rdy)
                        if tt > 0:
                            ng += 1
                        nip += 1
                win_g[kb] = snap("mmP")
                if not prefix:
                    wg_g[kb] = snap("mmG")
                if k + 2 < min(NCH, mixk):
                    load_chunk(k + 2)
            if pend_b[0] is not None:
                emit_stageB(pend_b[0])
                pend_b[0] = None
            if (not prefix) and pend[0] is not None:
                emit_ystage(pend[0])
                pend[0] = None

        if phases < 1:
            return nc
        with ExitStack() as c2:
            e2 = c2.enter_context
            xnT = e2(nc.sbuf_tensor("xnT", [128, KD * TH], BF16))
            ptrB = [e2(nc.psum_tensor(f"ptrB{i}", [128, 512], BF16)) for i in range(2)]
            ln_xt = [e2(nc.sbuf_tensor(f"ln_xt{i}", [128, D], F32)) for i in range(2)]
            ln_scrD = e2(nc.sbuf_tensor("ln_scrD", [128, D], F32))
            ln_scrA = e2(nc.sbuf_tensor("ln_scrA", [128, D], F32))
            ln_sm = [e2(nc.sbuf_tensor(f"ln_sm{i}", [128, 1], F32)) for i in range(2)]
            ln_sq = [e2(nc.sbuf_tensor(f"ln_sq{i}", [128, 1], F32)) for i in range(2)]
            ln_mv = [e2(nc.sbuf_tensor(f"ln_mv{i}", [128, 1], F32)) for i in range(2)]
            ln_qv = [e2(nc.sbuf_tensor(f"ln_qv{i}", [128, 1], F32)) for i in range(2)]
            ln_m2 = [e2(nc.sbuf_tensor(f"ln_m2{i}", [128, 1], F32)) for i in range(2)]
            ln_va = [e2(nc.sbuf_tensor(f"ln_va{i}", [128, 1], F32)) for i in range(2)]
            ln_lv = [e2(nc.sbuf_tensor(f"ln_lv{i}", [128, 1], F32)) for i in range(2)]
            ln_rs = [e2(nc.sbuf_tensor(f"ln_rs{i}", [128, 1], F32)) for i in range(2)]
            ln_xnb = [e2(nc.sbuf_tensor(f"ln_xnb{i}", [128, D], BF16)) for i in range(2)]
            lnbufs = (ln_xt, ln_scrD, ln_scrA, ln_sm, ln_sq, ln_mv, ln_qv,
                      ln_m2, ln_va, ln_lv, ln_rs, ln_xnb)

            emit_xnT(pre, xnT, ptrB, NBLK, lnbufs)
            nc.all_engine_barrier()
            if debug_taps and phases == 1:
                G.dma_start(out=dbg["xnT"][:, :], in_=xnT[:]).then_inc(
                    sems["dS"], 16)
                cnt["dS"] += 16
                wt(G, snap("dS"))
            if phases < 2:
                return nc

            with ExitStack() as c3:
                e3 = c3.enter_context
                winT = [e3(nc.sbuf_tensor(f"winT{i}", [128, KD * CCH], BF16))
                        for i in range(2)]
                wgT = [e3(nc.sbuf_tensor(f"wgT{i}", [128, KD * CCH], BF16))
                       for i in range(2)]
                proj_buf = [e3(nc.sbuf_tensor(f"projb{i}", [128, TH], F32))
                            for i in range(2)]
                cA = e3(nc.sbuf_tensor("cA", [128, 512], F32))
                cB = e3(nc.sbuf_tensor("cB", [128, 512], F32))
                cX = [e3(nc.sbuf_tensor(f"cX{i}", [128, 512], F32)) for i in range(2)]
                xtile = [e3(nc.sbuf_tensor(f"xtile{i}", [128, 512], F32))
                         for i in range(2)]
                xtile_b = [e3(nc.sbuf_tensor(f"xtileb{i}", [128, 512], BF16))
                           for i in range(2)]
                spM = [e3(nc.sbuf_tensor(f"spM{i}", [128, 512], F32)) for i in range(2)]
                spL = [e3(nc.sbuf_tensor(f"spL{i}", [128, 512], F32)) for i in range(2)]
                spN = [e3(nc.sbuf_tensor(f"spN{i}", [128, 512], F32)) for i in range(2)]
                dtb_sb = [e3(nc.sbuf_tensor(f"dtb{i}", [128, 512], F32))
                          for i in range(2)]
                a_buf = [e3(nc.sbuf_tensor(f"abuf{i}", [128, 513], F32))
                         for i in range(2)]
                u_buf = [e3(nc.sbuf_tensor(f"ubuf{i}", [128, 513], F32))
                         for i in range(2)]
                s_sb = [e3(nc.sbuf_tensor(f"ssb{i}", [128, 513], F32))
                        for i in range(2)]
                g_sb = [e3(nc.sbuf_tensor(f"gsb{i}", [128, 512], F32))
                        for i in range(2)]
                t1_sb = [e3(nc.sbuf_tensor(f"t1sb{i}", [128, 512], F32))
                         for i in range(2)]
                t2_sb = [e3(nc.sbuf_tensor(f"t2sb{i}", [128, 512], F32))
                         for i in range(2)]
                comb_sb = [e3(nc.sbuf_tensor(f"cmbsb{i}", [128, 512], BF16))
                           for i in range(2)]
                pp = [e3(nc.psum_tensor(f"pp{i}", [128, 512], F32)) for i in range(2)]
                pdt = [e3(nc.psum_tensor(f"pdt{i}", [128, 512], F32)) for i in range(2)]
                pq = [e3(nc.psum_tensor(f"pq{i}", [128, 512], F32)) for i in range(2)]
                for i in range(2):
                    V.memset(a_buf[i][:, 0:1], 0.0)
                nc.all_engine_barrier()

                mixbufs = (winT, wgT, pp, pq, pdt, proj_buf, cA, cB, cX,
                           xtile, xtile_b, spM, spL, spN, dtb_sb, a_buf, u_buf,
                           s_sb, g_sb, t1_sb, t2_sb, comb_sb)
                emit_mixer(xnT, True, mixbufs)
                nc.all_engine_barrier()
                if debug_taps and phases == 2:
                    for di, (buf, w) in enumerate([
                            (proj_buf[1], TH), (cX[1], 512), (xtile[1], 512),
                            (dtb_sb[1], 512), (a_buf[1], 513), (u_buf[1], 513),
                            (s_sb[1], 513)]):
                        G.dma_start(out=dbg["pipe"][di, :, 0:w],
                                    in_=buf[:, 0:w]).then_inc(sems["dS"], 16)
                        cnt["dS"] += 16
                    wt(G, snap("dS"))
                    nc.all_engine_barrier()
                gs0 = inc(V.tensor_scalar(out=s0[:], in0=spre[:],
                                          scalar1=p_sm[:, 0:1], scalar2=None,
                                          op0=ALU.mult), "vSm")
                if debug_taps:
                    wt(G, gs0)
                    G.dma_start(out=dbg["spre"][:, :], in_=s0[:]).then_inc(
                        sems["dS"], 16)
                    cnt["dS"] += 16
                    wt(G, snap("dS"))
                nc.all_engine_barrier()
                if phases < 3:
                    return nc

                emit_xnT(hid, xnT, ptrB, NBLK, lnbufs)
                nc.all_engine_barrier()
                if debug_taps:
                    G.dma_start(out=dbg["xnT"][:, :], in_=xnT[:]).then_inc(
                        sems["dS"], 16)
                    cnt["dS"] += 16
                    nc.all_engine_barrier()
                if phases < 4:
                    return nc

                emit_mixer(xnT, False, mixbufs)
                wt(SY, snap("dCmb"))
                nc.all_engine_barrier()
                if debug_taps:
                    for sI in range(NSUB):
                        G.dma_start(out=dbg["comb"][sI, :, :],
                                    in_=comb_d[sI, :, :]).then_inc(sems["dS"], 16)
                        cnt["dS"] += 16
                    wt(G, snap("dS"))
                    nc.all_engine_barrier()
        if phases < 5:
            return nc

        with ExitStack() as c5:
            e5 = c5.enter_context
            woT = e5(nc.sbuf_tensor("woT", [128, NSUB * 1024], BF16))
            cmb = [e5(nc.sbuf_tensor(f"cmb{i}", [128, NSUB * 128], BF16))
                   for i in range(2)]
            x_in = [e5(nc.sbuf_tensor(f"xin{i}", [128, 1024], F32)) for i in range(2)]
            x2h = [e5(nc.sbuf_tensor(f"x2h{i}", [128, 1024], F32)) for i in range(2)]
            po = [e5(nc.psum_tensor(f"po{i}", [128, 512], F32)) for i in range(2)]
            srco = w_oT.ap().rearrange("(a p) e -> p a e", p=128)
            cmb_g = [None, None]
            xin_g = [None, None]
            x2h_g = [None, None]
            po_g = [None, None]
            wo_g = None
            for eh in range(2):
                wt(G, wo_g)
                gwo = inc(G.dma_start(
                    out=woT[:].rearrange("p (a e) -> p a e", e=1024),
                    in_=srco[:, :, eh * 1024:(eh + 1) * 1024]), "dWG", 16)
                for i in range(16):
                    par = i % 2
                    wt(SY, cmb_g[par])
                    gcm = inc(SY.dma_start(
                        out=cmb[par][:].rearrange("p (a t) -> p a t", t=128),
                        in_=comb_d.ap().rearrange("s p t -> p s t")[
                            :, :, i * 128:(i + 1) * 128]), "dX", 16)
                    wt(SY, xin_g[par])
                    gxi = inc(SY.dma_start(
                        out=x_in[par][:],
                        in_=hid[128 + i * 128:256 + i * 128,
                                eh * 1024:(eh + 1) * 1024]), "dX", 16)
                    wt(P, gwo)
                    wt(P, gcm)
                    for et in range(2):
                        wt(P, po_g[et])
                        for sI in range(NSUB):
                            mm = P.matmul(
                                po[et][:],
                                cmb[par][:, sI * 128:(sI + 1) * 128],
                                woT[:].rearrange("p (a e) -> p a e", e=1024)[
                                    :, sI, et * 512:(et + 1) * 512],
                                start=(sI == 0), stop=(sI == NSUB - 1))
                        gmo = inc(mm, "mmP")
                        if et == 1:
                            cmb_g[par] = gmo
                        wt(V, gmo)
                        wt(V, gxi)
                        wt(V, x2h_g[par] if et == 0 else None)
                        gev = inc(V.tensor_add(x2h[par][:, et * 512:(et + 1) * 512],
                                               po[et][:],
                                               x_in[par][:, et * 512:(et + 1) * 512]),
                                  "vEvO")
                        po_g[et] = gev
                        if et == 1:
                            xin_g[par] = gev
                    wt(G, gev)
                    x2h_g[par] = inc(G.dma_start(
                        out=x2_d[i, :, eh * 1024:(eh + 1) * 1024],
                        in_=x2h[par][:]), "dS", 16)
                wo_g = snap("mmP")
            wt(SY, snap("dS"))
            nc.all_engine_barrier()
            if debug_taps:
                for i in range(16):
                    G.dma_start(out=dbg["x2"][i, :, :], in_=x2_d[i, :, :]).then_inc(
                        sems["dS"], 16)
                    cnt["dS"] += 16
                wt(G, snap("dS"))
                nc.all_engine_barrier()
        if phases < 6:
            return nc

        with ExitStack() as c6o:
            e6o = c6o.enter_context
            xn2qT = e6o(nc.sbuf_tensor("xn2qT", [128, KD * T], BF16))

            with ExitStack() as c6:
                e6 = c6.enter_context
                xt = [e6(nc.sbuf_tensor(f"f2xt{i}", [128, D], F32)) for i in range(2)]
                scrD = e6(nc.sbuf_tensor("f2scrD", [128, D], F32))
                scrA = e6(nc.sbuf_tensor("f2scrA", [128, D], F32))
                xn2 = [e6(nc.sbuf_tensor(f"xn2_{i}", [128, D], F32)) for i in range(2)]
                am = e6(nc.sbuf_tensor("f2am", [128, 64], F32))
                rs2 = e6(nc.sbuf_tensor("f2rs", [128, 64], F32))
                rs2b = e6(nc.sbuf_tensor("f2rsb", [128, 64], F32))
                rs2c = e6(nc.sbuf_tensor("f2rsc", [128, 64], F32))
                rs2f = e6(nc.sbuf_tensor("f2rsf", [128, 64], F32))
                ts2f = e6(nc.sbuf_tensor("f2tsf", [128, 64], F32))
                ts2 = e6(nc.sbuf_tensor("f2ts", [128, 64], F32))
                q8 = e6(nc.sbuf_tensor("f2q8", [128, D], F8E4))
                xq = [e6(nc.sbuf_tensor(f"f2xq{i}", [128, D], BF16)) for i in range(2)]
                sm1 = [e6(nc.sbuf_tensor(f"f2sm{i}", [128, 1], F32)) for i in range(2)]
                sq1 = [e6(nc.sbuf_tensor(f"f2sq{i}", [128, 1], F32)) for i in range(2)]
                mv = [e6(nc.sbuf_tensor(f"f2mv{i}", [128, 1], F32)) for i in range(2)]
                qv = [e6(nc.sbuf_tensor(f"f2qv{i}", [128, 1], F32)) for i in range(2)]
                m2 = [e6(nc.sbuf_tensor(f"f2m2{i}", [128, 1], F32)) for i in range(2)]
                va = [e6(nc.sbuf_tensor(f"f2va{i}", [128, 1], F32)) for i in range(2)]
                lv = [e6(nc.sbuf_tensor(f"f2lv{i}", [128, 1], F32)) for i in range(2)]
                rst = [e6(nc.sbuf_tensor(f"f2rst{i}", [128, 1], F32)) for i in range(2)]
                ptrF = [e6(nc.psum_tensor(f"ptrF{i}", [128, 512], BF16))
                        for i in range(2)]
                xt_g = [None, None]
                xq_g = [None, None]
                ptr_g = [None, None]
                rb_g = [None]
                f2f_g = [None]
                for blk in range(16):
                    par = blk % 2
                    wt(SY, xt_g[par])
                    gld = inc(SY.dma_start(out=xt[par][:], in_=x2_d[blk, :, :]),
                              "dX", 16)
                    wt(V, gld)
                    gsum = inc(V.tensor_scalar(out=scrD[:], in0=xt[par][:],
                                               scalar1=1.0, scalar2=0.0,
                                               op0=ALU.mult, op1=ALU.add,
                                               accum_out=sm1[par][:, 0:1]), "vSm")
                    wt(A, gld)
                    A.activation(out=scrA[:], in_=xt[par][:], func=AF.Square,
                                 accum_out=sq1[par][:, 0:1])
                    wt(A, gsum)
                    A.activation(out=mv[par][:], in_=sm1[par][:], func=AF.Copy,
                                 scale=1.0 / 2048.0)
                    A.activation(out=qv[par][:], in_=sq1[par][:], func=AF.Copy,
                                 scale=1.0 / 2048.0)
                    gm2 = inc(A.activation(out=m2[par][:], in_=mv[par][:],
                                           func=AF.Square), "aSm")
                    wt(V, gm2)
                    gvar = inc(V.scalar_tensor_tensor(
                        out=va[par][:], in0=qv[par][:], scalar=1e-5,
                        in1=m2[par][:], op0=ALU.add, op1=ALU.subtract), "vSm")
                    wt(A, gvar)
                    A.activation(out=lv[par][:], in_=va[par][:], func=AF.Ln)
                    grst = inc(A.activation(out=rst[par][:], in_=lv[par][:],
                                            func=AF.Exp, scale=-0.5), "aSm")
                    wt(V, grst)
                    V.tensor_scalar(out=xn2[par][:], in0=xt[par][:],
                                    scalar1=mv[par][:, 0:1],
                                    scalar2=rst[par][:, 0:1],
                                    op0=ALU.subtract, op1=ALU.mult)
                    V.tensor_reduce(out=am[:, :, None],
                                    in_=xn2[par][:].rearrange("p (b k) -> p b k", k=32),
                                    axis=AX.X, op=ALU.max, apply_absolute_value=True)
                    wt(V, rb_g[0])
                    grc = inc(V.reciprocal(out=rs2b[:], in_=am[:]), "vSm")
                    wt(A, grc)
                    gcp = inc(A.activation(out=rs2c[:], in_=rs2b[:], func=AF.Copy),
                              "aSm")
                    rb_g[0] = gcp
                    wt(V, gcp)
                    V.tensor_scalar(out=rs2[:], in0=rs2c[:], scalar1=224.0,
                                    scalar2=4.48e31, op0=ALU.mult, op1=ALU.min)
                    gsc = inc(V.tensor_scalar(out=ts2[:], in0=am[:],
                                              scalar1=1.0 / 224.0, scalar2=2e-30,
                                              op0=ALU.mult, op1=ALU.max), "vSm")
                    wt(A, gsc)
                    wt(A, f2f_g[0])
                    A.activation(out=rs2f[:], in_=rs2[:], func=AF.Copy)
                    A.activation(out=ts2f[:], in_=ts2[:], func=AF.Copy)
                    gfc = inc(A.activation(out=stl[:], in_=ts2f[:, 0:1],
                                           func=AF.Copy), "aSm")
                    wt(V, gfc)
                    V.tensor_tensor(out=q8[:].rearrange("p (b k) -> p b k", k=32),
                                    in0=xn2[par][:].rearrange("p (b k) -> p b k", k=32),
                                    in1=rs2f[:, :, None].broadcast_to([128, 64, 32]),
                                    op=ALU.mult)
                    wt(V, xq_g[par])
                    gxq = inc(V.tensor_tensor(
                        out=xq[par][:].rearrange("p (b k) -> p b k", k=32),
                        in0=q8[:].rearrange("p (b k) -> p b k", k=32),
                        in1=ts2f[:, :, None].broadcast_to([128, 64, 32]),
                        op=ALU.mult), "vXn")
                    f2f_g[0] = gxq
                    xt_g[par] = gxq
                    evg = None
                    for q in range(4):
                        pq_ = (blk * 4 + q) % 2
                        wt(P, gxq)
                        wt(P, ptr_g[pq_])
                        for u in range(4):
                            kd = q * 4 + u
                            mm = P.transpose(ptrF[pq_][:, u * 128:(u + 1) * 128],
                                             xq[par][:, kd * 128:(kd + 1) * 128],
                                             p_eye[:])
                        gtr = inc(mm, "mmT")
                        wt(A, gtr)
                        dst = xn2qT[:].rearrange("p (a t) -> p a t", t=T)[
                            :, q * 4:(q + 1) * 4, blk * 128:(blk + 1) * 128]
                        evg = inc(A.activation(out=dst, in_=ptrF[pq_][:].rearrange(
                            "p (a t) -> p a t", t=128), func=AF.Copy), "aTrE")
                        ptr_g[pq_] = evg
                    xq_g[par] = evg
                nc.all_engine_barrier()
                if debug_taps:
                    for di, (buf, w) in enumerate([
                            (am, 64), (rs2, 64), (ts2, 64), (xn2[1], D), (q8, D)]):
                        G.dma_start(out=dbg["pipe2"][di, :, 0:w],
                                    in_=buf[:, 0:w]).then_inc(sems["dS"], 16)
                        cnt["dS"] += 16
                    G.dma_start(out=dbg["xn2qT"][:, :], in_=xn2qT[:]).then_inc(
                        sems["dS"], 16)
                    cnt["dS"] += 16
                    wt(G, snap("dS"))
                    nc.all_engine_barrier()
            if phases < 7:
                return nc

            with ExitStack() as c7:
                e7 = c7.enter_context
                nat = [e7(nc.sbuf_tensor(f"nat{i}", [128, D], F32)) for i in range(2)]
                amw = e7(nc.sbuf_tensor("amw", [128, 64], F32))
                rsw = e7(nc.sbuf_tensor("rsw", [128, 64], F32))
                rswb = e7(nc.sbuf_tensor("rswb", [128, 64], F32))
                rswc = e7(nc.sbuf_tensor("rswc", [128, 64], F32))
                rswf = e7(nc.sbuf_tensor("rswf", [128, 64], F32))
                tswf = e7(nc.sbuf_tensor("tswf", [128, 64], F32))
                tsw = e7(nc.sbuf_tensor("tsw", [128, 64], F32))
                q8w = e7(nc.sbuf_tensor("q8w", [128, D], F8E4))
                wqb = e7(nc.sbuf_tensor("wqb", [128, D], BF16))
                wTg = [e7(nc.sbuf_tensor(f"wTg{i}", [128, KD * 512], BF16))
                       for i in range(2)]
                wTu = [e7(nc.sbuf_tensor(f"wTu{i}", [128, KD * 512], BF16))
                       for i in range(2)]
                sg = e7(nc.sbuf_tensor("sg", [128, 512], F32))
                hh4 = e7(nc.sbuf_tensor("hh4", [128, 2048], F32))
                amh = e7(nc.sbuf_tensor("amh", [128, 64], F32))
                rsh = e7(nc.sbuf_tensor("rsh", [128, 64], F32))
                rshb = e7(nc.sbuf_tensor("rshb", [128, 64], F32))
                rshc = e7(nc.sbuf_tensor("rshc", [128, 64], F32))
                rshf = e7(nc.sbuf_tensor("rshf", [128, 64], F32))
                tshf = e7(nc.sbuf_tensor("tshf", [128, 64], F32))
                tsh = e7(nc.sbuf_tensor("tsh", [128, 64], F32))
                q8h4 = e7(nc.sbuf_tensor("q8h4", [128, 2048], F8E4))
                hq4 = e7(nc.sbuf_tensor("hq4", [128, 2048], BF16))
                hqT = e7(nc.sbuf_tensor("hqT", [128, 4 * T], BF16))
                pg = [e7(nc.psum_tensor(f"pg{i}", [128, 512], F32)) for i in range(2)]
                pu = [e7(nc.psum_tensor(f"pu{i}", [128, 512], F32)) for i in range(2)]
                ptrG = [e7(nc.psum_tensor(f"ptrG{i}", [128, 512], BF16))
                        for i in range(2)]
                ptrH2 = [e7(nc.psum_tensor(f"ptrH2_{i}", [128, 512], BF16))
                         for i in range(2)]
                nat_g = [None, None]
                wT_ld = {}
                wT_g = [None, None]
                pg_g = [None, None]
                pu_g = [None, None]
                ptr_g = [None, None]
                ptr2_g = [None, None]
                hq_g = None
                hqT_g = None
                sg_g = None
                wqb_g = [None]
                rbw_g = [None]
                rbh_g = [None]
                gwf_g = [None]
                ghf_g = [None]
                hh4_g = None
                npiece = [0]

                def quant_piece(dst_wT, src_dram, ic, piece, dst_guard=None):
                    par = npiece[0] % 2
                    npiece[0] += 1
                    wt(SY, nat_g[par])
                    gld = inc(SY.dma_start(
                        out=nat[par][:],
                        in_=src_dram[(ic * 4 + piece) * 128:
                                     (ic * 4 + piece + 1) * 128, :]), "dW", 16)
                    wt(V, gld)
                    V.tensor_reduce(out=amw[:, :, None],
                                    in_=nat[par][:].rearrange("p (b k) -> p b k", k=32),
                                    axis=AX.X, op=ALU.max,
                                    apply_absolute_value=True)
                    wt(V, rbw_g[0])
                    grc = inc(V.reciprocal(out=rswb[:], in_=amw[:]), "vSm")
                    wt(A, grc)
                    gcp = inc(A.activation(out=rswc[:], in_=rswb[:], func=AF.Copy),
                              "aSm")
                    rbw_g[0] = gcp
                    wt(V, gcp)
                    V.tensor_scalar(out=rsw[:], in0=rswc[:], scalar1=224.0,
                                    scalar2=4.48e31, op0=ALU.mult, op1=ALU.min)
                    gsc = inc(V.tensor_scalar(out=tsw[:], in0=amw[:],
                                              scalar1=1.0 / 224.0, scalar2=2e-30,
                                              op0=ALU.mult, op1=ALU.max), "vSm")
                    wt(A, gsc)
                    wt(A, gwf_g[0])
                    A.activation(out=rswf[:], in_=rsw[:], func=AF.Copy)
                    A.activation(out=tswf[:], in_=tsw[:], func=AF.Copy)
                    gfc = inc(A.activation(out=stl[:], in_=tswf[:, 0:1],
                                           func=AF.Copy), "aSm")
                    wt(V, gfc)
                    V.tensor_tensor(out=q8w[:].rearrange("p (b k) -> p b k", k=32),
                                    in0=nat[par][:].rearrange("p (b k) -> p b k", k=32),
                                    in1=rswf[:, :, None].broadcast_to([128, 64, 32]),
                                    op=ALU.mult)
                    wt(V, wqb_g[0])
                    gwq = inc(V.tensor_tensor(
                        out=wqb[:].rearrange("p (b k) -> p b k", k=32),
                        in0=q8w[:].rearrange("p (b k) -> p b k", k=32),
                        in1=tswf[:, :, None].broadcast_to([128, 64, 32]),
                        op=ALU.mult), "vXn")
                    gwf_g[0] = gwq
                    nat_g[par] = gwq
                    evg = None
                    for q in range(4):
                        pq_ = q % 2
                        wt(P, gwq)
                        wt(P, ptr_g[pq_])
                        for u in range(4):
                            kd = q * 4 + u
                            mm = P.transpose(ptrG[pq_][:, u * 128:(u + 1) * 128],
                                             wqb[:, kd * 128:(kd + 1) * 128],
                                             p_eye[:])
                        gtr = inc(mm, "mmT")
                        if q == 3:
                            wqb_g[0] = gtr
                        wt(A, gtr)
                        if q == 0 and piece == 0:
                            wt(A, dst_guard)
                        dst = dst_wT[:].rearrange("p (a c) -> p a c", c=512)[
                            :, q * 4:(q + 1) * 4, piece * 128:(piece + 1) * 128]
                        evg = inc(A.activation(out=dst, in_=ptrG[pq_][:].rearrange(
                            "p (a t) -> p a t", t=128), func=AF.Copy), "aTrE")
                        ptr_g[pq_] = evg
                    return evg

                wT_use = [None, None]
                for piece in range(4):
                    wT_ld[("g", 0)] = quant_piece(wTg[0], w_fg.ap(), 0, piece)
                    wT_ld[("u", 0)] = quant_piece(wTu[0], w_fu.ap(), 0, piece)
                for ic in range(16):
                    icp = ic % 2
                    prep = []
                    if ic + 1 < 16:
                        for piece in range(4):
                            prep.append((wTg[(ic + 1) % 2], w_fg.ap(), piece, "g"))
                            prep.append((wTu[(ic + 1) % 2], w_fu.ap(), piece, "u"))
                    for i in range(16):
                        gp_ = (ic * 16 + i) % 2
                        wt(P, wT_ld[("g", ic)])
                        wt(P, pg_g[gp_])
                        for kd in range(KD):
                            mm = P.matmul(
                                pg[gp_][:],
                                xn2qT[:].rearrange("p (a t) -> p a t", t=T)[
                                    :, kd, i * 128:(i + 1) * 128],
                                wTg[icp][:].rearrange("p (a c) -> p a c", c=512)[
                                    :, kd, :],
                                start=(kd == 0), stop=(kd == KD - 1))
                        gmg = inc(mm, "mmG")
                        wt(P, wT_ld[("u", ic)])
                        wt(P, pu_g[gp_])
                        for kd in range(KD):
                            mm = P.matmul(
                                pu[gp_][:],
                                xn2qT[:].rearrange("p (a t) -> p a t", t=T)[
                                    :, kd, i * 128:(i + 1) * 128],
                                wTu[icp][:].rearrange("p (a c) -> p a c", c=512)[
                                    :, kd, :],
                                start=(kd == 0), stop=(kd == KD - 1))
                        gmu = inc(mm, "mmG")
                        wt(A, gmg)
                        wt(A, sg_g)
                        gsl = inc(A.activation(out=sg[:], in_=pg[gp_][:],
                                               func=AF.Silu), "aSig")
                        pg_g[gp_] = gsl
                        wt(V, gmu)
                        wt(V, gsl)
                        wt(V, hh4_g if i % 4 == 0 else None)
                        gh = inc(V.tensor_mul(hh4[:, (i % 4) * 512:(i % 4 + 1) * 512],
                                              sg[:], pu[gp_][:]), "vCmb")
                        pu_g[gp_] = gh
                        sg_g = gh
                        if i % 4 == 3:
                            V.tensor_reduce(out=amh[:, :, None],
                                            in_=hh4[:].rearrange("p (b k) -> p b k", k=32),
                                            axis=AX.X, op=ALU.max,
                                            apply_absolute_value=True)
                            wt(V, rbh_g[0])
                            grc = inc(V.reciprocal(out=rshb[:], in_=amh[:]), "vSm")
                            wt(A, grc)
                            gcp = inc(A.activation(out=rshc[:], in_=rshb[:],
                                                   func=AF.Copy), "aSm")
                            rbh_g[0] = gcp
                            wt(V, gcp)
                            V.tensor_scalar(out=rsh[:], in0=rshc[:], scalar1=224.0,
                                            scalar2=4.48e31, op0=ALU.mult,
                                            op1=ALU.min)
                            gsc = inc(V.tensor_scalar(out=tsh[:], in0=amh[:],
                                                      scalar1=1.0 / 224.0,
                                                      scalar2=2e-30, op0=ALU.mult,
                                                      op1=ALU.max), "vSm")
                            wt(A, gsc)
                            wt(A, ghf_g[0])
                            A.activation(out=rshf[:], in_=rsh[:], func=AF.Copy)
                            A.activation(out=tshf[:], in_=tsh[:], func=AF.Copy)
                            gfc = inc(A.activation(out=stl[:], in_=tshf[:, 0:1],
                                                   func=AF.Copy), "aSm")
                            wt(V, gfc)
                            V.tensor_tensor(
                                out=q8h4[:].rearrange("p (b k) -> p b k", k=32),
                                in0=hh4[:].rearrange("p (b k) -> p b k", k=32),
                                in1=rshf[:, :, None].broadcast_to([128, 64, 32]),
                                op=ALU.mult)
                            wt(V, hq_g)
                            ghq = inc(V.tensor_tensor(
                                out=hq4[:].rearrange("p (b k) -> p b k", k=32),
                                in0=q8h4[:].rearrange("p (b k) -> p b k", k=32),
                                in1=tshf[:, :, None].broadcast_to([128, 64, 32]),
                                op=ALU.mult), "vXb")
                            ghf_g[0] = ghq
                            hh4_g = ghq
                            for ir in range(4):
                                i0 = i - 3 + ir
                                pq_ = ir % 2
                                wt(P, ghq)
                                wt(P, ptr2_g[pq_])
                                for u in range(4):
                                    mm = P.transpose(
                                        ptrH2[pq_][:, u * 128:(u + 1) * 128],
                                        hq4[:, ir * 512 + u * 128:
                                            ir * 512 + (u + 1) * 128], p_eye[:])
                                gtr = inc(mm, "mmT")
                                hq_g = gtr
                                wt(A, gtr)
                                wt(A, hqT_g)
                                dst = hqT[:].rearrange("p (a t) -> p a t", t=T)[
                                    :, :, i0 * 128:(i0 + 1) * 128]
                                evg = inc(A.activation(
                                    out=dst, in_=ptrH2[pq_][:].rearrange(
                                        "p (a t) -> p a t", t=128),
                                    func=AF.Copy), "aTrE")
                                ptr2_g[pq_] = evg
                        if INTERLEAVE_PREP and i >= 8 and prep:
                            dstw, srcw, piece, tag = prep.pop(0)
                            gq = quant_piece(dstw, srcw, ic + 1, piece,
                                             dst_guard=wT_use[(ic + 1) % 2])
                            wT_ld[(tag, ic + 1)] = gq
                    if not INTERLEAVE_PREP:
                        while prep:
                            dstw, srcw, piece, tag = prep.pop(0)
                            gq = quant_piece(dstw, srcw, ic + 1, piece,
                                             dst_guard=wT_use[(ic + 1) % 2])
                            wT_ld[(tag, ic + 1)] = gq
                    wT_use[icp] = snap("mmG")
                    wt(G, snap("aTrE"))
                    for u in range(4):
                        hqT_g = inc(G.dma_start(out=hqT_d[ic * 4 + u, :, :],
                                                in_=hqT[:, u * T:(u + 1) * T]),
                                    "dS", 16)
                wt(SY, snap("dS"))
                nc.all_engine_barrier()
                if debug_taps:
                    for di, (buf, w) in enumerate([(hh4, 2048), (q8h4, 2048),
                                                   (hq4, 2048), (sg, 512),
                                                   (tsh, 64)]):
                        G.dma_start(out=dbg["pipe2"][di, :, 0:w],
                                    in_=buf[:, 0:w]).then_inc(sems["dS"], 16)
                        cnt["dS"] += 16
                    for sI in range(I // 128):
                        G.dma_start(out=dbg["hqT"][sI, :, :],
                                    in_=hqT_d[sI, :, :]).then_inc(sems["dS"], 16)
                        cnt["dS"] += 16
                    wt(G, snap("dS"))
                    nc.all_engine_barrier()
        if phases < 8:
            return nc

        with ExitStack() as c8:
            e8 = c8.enter_context
            natd = [e8(nc.sbuf_tensor(f"natd{i}", [128, 2048], F32)) for i in range(2)]
            amd = e8(nc.sbuf_tensor("amd", [128, 64], F32))
            rsd = e8(nc.sbuf_tensor("rsd", [128, 64], F32))
            rsdb = e8(nc.sbuf_tensor("rsdb", [128, 64], F32))
            rsdc = e8(nc.sbuf_tensor("rsdc", [128, 64], F32))
            rsdf = e8(nc.sbuf_tensor("rsdf", [128, 64], F32))
            tsdf = e8(nc.sbuf_tensor("tsdf", [128, 64], F32))
            tsd = e8(nc.sbuf_tensor("tsd", [128, 64], F32))
            q8d = e8(nc.sbuf_tensor("q8d", [128, 2048], F8E4))
            wqd = e8(nc.sbuf_tensor("wqd", [128, 2048], BF16))
            wTd = e8(nc.sbuf_tensor("wTd", [128, 64 * 512], BF16))
            hqt = [e8(nc.sbuf_tensor(f"hqt{i}", [128, 64 * 128], BF16))
                   for i in range(2)]
            x2t = [e8(nc.sbuf_tensor(f"x2t{i}", [128, 512], F32)) for i in range(2)]
            of = [e8(nc.sbuf_tensor(f"of{i}", [128, 512], F32)) for i in range(2)]
            pd = [e8(nc.psum_tensor(f"pd{i}", [128, 512], F32)) for i in range(2)]
            ptrH = [e8(nc.psum_tensor(f"ptrH{i}", [128, 512], BF16)) for i in range(2)]
            natd_g = [None, None]
            hqt_g = [None, None]
            x2t_g = [None, None]
            of_g = [None, None]
            pd_g = [None, None]
            ptr_g = [None, None]
            wTd_g = None
            wqd_g = [None]
            rbd_g = [None]
            ghd_g = [None]
            for eh in range(4):
                for pc in range(16):
                    eblk = pc // 4
                    ipc = pc % 4
                    par = pc % 2
                    erow = eh * 512 + eblk * 128
                    wt(SY, natd_g[par])
                    gld = inc(SY.dma_start(
                        out=natd[par][:],
                        in_=w_fd[erow:erow + 128, ipc * 2048:(ipc + 1) * 2048]),
                        "dW", 16)
                    wt(V, gld)
                    V.tensor_reduce(out=amd[:, :, None],
                                    in_=natd[par][:].rearrange("p (b k) -> p b k", k=32),
                                    axis=AX.X, op=ALU.max,
                                    apply_absolute_value=True)
                    wt(V, rbd_g[0])
                    grc = inc(V.reciprocal(out=rsdb[:], in_=amd[:]), "vSm")
                    wt(A, grc)
                    gcp = inc(A.activation(out=rsdc[:], in_=rsdb[:], func=AF.Copy),
                              "aSm")
                    rbd_g[0] = gcp
                    wt(V, gcp)
                    V.tensor_scalar(out=rsd[:], in0=rsdc[:], scalar1=224.0,
                                    scalar2=4.48e31, op0=ALU.mult, op1=ALU.min)
                    gsc = inc(V.tensor_scalar(out=tsd[:], in0=amd[:],
                                              scalar1=1.0 / 224.0, scalar2=2e-30,
                                              op0=ALU.mult, op1=ALU.max), "vSm")
                    wt(A, gsc)
                    wt(A, ghd_g[0])
                    A.activation(out=rsdf[:], in_=rsd[:], func=AF.Copy)
                    A.activation(out=tsdf[:], in_=tsd[:], func=AF.Copy)
                    gfc = inc(A.activation(out=stl[:], in_=tsdf[:, 0:1],
                                           func=AF.Copy), "aSm")
                    wt(V, gfc)
                    V.tensor_tensor(out=q8d[:].rearrange("p (b k) -> p b k", k=32),
                                    in0=natd[par][:].rearrange("p (b k) -> p b k", k=32),
                                    in1=rsdf[:, :, None].broadcast_to([128, 64, 32]),
                                    op=ALU.mult)
                    wt(V, wqd_g[0])
                    gwq = inc(V.tensor_tensor(
                        out=wqd[:].rearrange("p (b k) -> p b k", k=32),
                        in0=q8d[:].rearrange("p (b k) -> p b k", k=32),
                        in1=tsdf[:, :, None].broadcast_to([128, 64, 32]),
                        op=ALU.mult), "vXn")
                    ghd_g[0] = gwq
                    natd_g[par] = gwq
                    for q in range(4):
                        pq_ = q % 2
                        wt(P, gwq)
                        wt(P, ptr_g[pq_])
                        for u in range(4):
                            kd = q * 4 + u
                            mm = P.transpose(ptrH[pq_][:, u * 128:(u + 1) * 128],
                                             wqd[:, kd * 128:(kd + 1) * 128],
                                             p_eye[:])
                        gtr = inc(mm, "mmT")
                        if q == 3:
                            wqd_g[0] = gtr
                        wt(A, gtr)
                        if pc == 0 and q == 0:
                            wt(A, wTd_g)
                        isub = ipc * 16 + q * 4
                        dst = wTd[:].rearrange("p (a e) -> p a e", e=512)[
                            :, isub:isub + 4, eblk * 128:(eblk + 1) * 128]
                        evg = inc(A.activation(out=dst, in_=ptrH[pq_][:].rearrange(
                            "p (a t) -> p a t", t=128), func=AF.Copy), "aTrE")
                        ptr_g[pq_] = evg
                wprep = snap("aTrE")
                for i in range(16):
                    par = i % 2
                    wt(SY, hqt_g[par])
                    ghl = inc(SY.dma_start(
                        out=hqt[par][:].rearrange("p (a t) -> p a t", t=128),
                        in_=hqT_d.ap().rearrange("a p t -> p a t")[
                            :, :, i * 128:(i + 1) * 128]), "dX", 16)
                    wt(SY, x2t_g[par])
                    gxl = inc(SY.dma_start(
                        out=x2t[par][:],
                        in_=x2_d[i, :, eh * 512:(eh + 1) * 512]), "dX", 16)
                    wt(P, ghl)
                    wt(P, wprep)
                    wt(P, pd_g[par])
                    for sI in range(64):
                        mm = P.matmul(
                            pd[par][:],
                            hqt[par][:, sI * 128:(sI + 1) * 128],
                            wTd[:].rearrange("p (a e) -> p a e", e=512)[:, sI, :],
                            start=(sI == 0), stop=(sI == 63))
                    gmd = inc(mm, "mmD")
                    hqt_g[par] = gmd
                    wt(V, gmd)
                    wt(V, gxl)
                    wt(V, of_g[par])
                    gev = inc(V.tensor_add(of[par][:], pd[par][:], x2t[par][:]),
                              "vEvO")
                    pd_g[par] = gev
                    x2t_g[par] = gev
                    wt(G, gev)
                    of_g[par] = inc(G.dma_start(
                        out=out[i * 128:(i + 1) * 128, eh * 512:(eh + 1) * 512],
                        in_=of[par][:]), "oDone", 16)
                wTd_g = snap("mmD")
            wt(G, snap("oDone"))

    return nc


def _layout_params(ins):
    f = np.float32
    cw = np.asarray(ins["L0_attn_conv_weight"], f)
    cb = np.asarray(ins["L0_attn_conv_bias"], f)
    dtw = np.asarray(ins["L0_attn_dt_w"], f)
    dtb = np.asarray(ins["L0_attn_dt_b"], f)
    al = np.asarray(ins["L0_attn_A_log"], f)
    B = np.asarray(ins["L0_attn_B"], f)
    Cc = np.asarray(ins["L0_attn_C"], f)
    Dc = np.asarray(ins["L0_attn_D"], f)

    def chanv(v):
        return np.ascontiguousarray(v.reshape(32, 128).T)

    cw4 = np.ascontiguousarray(
        cw.reshape(32, 128, 4).transpose(1, 0, 2).reshape(128, 128))
    dtwf = dtw.reshape(C)
    dt2 = np.zeros((128, 32 * 128), f)
    for s in range(32):
        col = dtwf[s * 128:(s + 1) * 128]
        blk = np.zeros((128, 128), f)
        blk[0:64, 0:64] = col[0:64, None]
        blk[64:128, 64:128] = col[64:128, None]
        dt2[:, s * 128:(s + 1) * 128] = blk
    return dict(
        cw4=cw4, cbv=chanv(cb), dt2=dt2, dtbc=chanv(np.repeat(dtb, 64)),
        alog=chanv(al.reshape(C)), bcv=chanv(B.reshape(C)),
        ccv=chanv(Cc.reshape(C)), dcv=chanv(Dc.reshape(C)),
        eye=np.eye(128, dtype=f),
    )


_NC_CACHE = {}
_LAST_RES = None


def kernel(**ins):
    x = np.asarray(ins["L0_attn_hidden_states"], np.float32)
    f = np.float32
    weights = dict(
        w_inT=np.ascontiguousarray(np.asarray(ins["L0_attn_in_proj_w"], f).T),
        w_gT=np.ascontiguousarray(np.asarray(ins["L0_attn_gate_proj_w"], f).T),
        w_oT=np.ascontiguousarray(np.asarray(ins["L0_attn_out_proj_w"], f).T),
        w_fg=np.ascontiguousarray(np.asarray(ins["L0_ffn_gate_w"], f)),
        w_fu=np.ascontiguousarray(np.asarray(ins["L0_ffn_up_w"], f)),
        w_fd=np.ascontiguousarray(np.asarray(ins["L0_ffn_down_w"], f)),
    )
    params = _layout_params(ins)

    key = ("prog", DEBUG_TAPS, PHASES, MIXK, MIXSTAGE)
    if key not in _NC_CACHE:
        _NC_CACHE[key] = build_program(DEBUG_TAPS, PHASES, MIXK, MIXSTAGE)
    nc = _NC_CACHE[key]

    in_maps = []
    for core in range(8):
        b, half = core // 2, core % 2
        seq = x[b]
        hid_a = np.zeros((TH, D), f)
        lo = half * 2048 - 128
        hid_a[max(0, -lo):, :] = seq[max(lo, 0): half * 2048 + 2048, :]
        pre_a = np.zeros((TH, D), f)
        if half == 1:
            pre_a[128:, :] = seq[0:2048, :]
        m = dict(hid=np.ascontiguousarray(hid_a), pre=np.ascontiguousarray(pre_a),
                 smask=np.full((128, 1), float(half), f))
        m.update(weights)
        m.update(params)
        in_maps.append(m)

    res = run_bass_kernel_spmd(nc, in_maps, core_ids=list(range(8)))
    global _LAST_RES
    _LAST_RES = res
    outp = np.empty((4, 4096, 2048), np.float32)
    for core in range(8):
        b, half = core // 2, core % 2
        outp[b, half * 2048:(half + 1) * 2048, :] = res.results[core]["out"]
    return outp


if __name__ == "__main__":
    t_nc = build_program(DEBUG_TAPS)
    print("built:", t_nc.n_instructions(), "instructions")
